# version 02 (final)
# baseline (speedup 1.0000x reference)
import math
from contextlib import ExitStack

import numpy as np
import ml_dtypes
import concourse.bass as bass
import concourse.mybir as mybir
from concourse.bass_utils import run_bass_kernel_spmd
from concourse.aot_env import allow_ici_all_to_all

F32 = mybir.dt.float32
BF16 = mybir.dt.bfloat16
I32 = mybir.dt.int32
AF = mybir.ActivationFunctionType
ALU = mybir.AluOpType
AX = mybir.AxisListType

NCORES = 8
B_PER = 2
T = 2048
D = 1024
NTC = T // 128
NDC = D // 128

ENGINES = ("sync", "vector", "scalar", "tensor", "gpsimd")

DEV_LAYERS = (0, 1, 2, 3)
L1_BRANCHES = (0, 1, 2)
L3_PARTS = ("attn", "ffn")
L2_PARTS = ("attn", "ffn")
DEV_DBG = False
DEV_NCORES = NCORES


class Tok:
    __slots__ = ("sem", "val")

    def __init__(self, sem, val):
        self.sem = sem
        self.val = val


class Prog:
    def __init__(self):
        self.steps = []
        self.cnt = {e: 0 for e in ENGINES}
        self.extra_sems = []
        self.waited = {e: {} for e in ENGINES}

    def new_sem(self, name):
        assert name not in self.cnt
        self.cnt[name] = 0
        self.extra_sems.append(name)
        return name

    def step(self, eng, fn, deps=(), dma=False, sem=None):
        waits = []
        for t in deps:
            if t is None:
                continue
            if t.sem == eng == "tensor":
                continue
            w = self.waited[eng]
            if w.get(t.sem, -1) < t.val:
                waits.append((t.sem, t.val))
                w[t.sem] = t.val
        inc_sem = sem if sem is not None else eng
        inc = 16 if dma else 1
        self.cnt[inc_sem] += inc
        self.steps.append((eng, fn, waits, inc_sem, inc))
        return Tok(inc_sem, self.cnt[inc_sem])

    def barrier_tok(self, eng):
        return Tok(eng, self.cnt[eng])

    def barrier_all(self):
        return [Tok(s, c) for s, c in self.cnt.items() if c > 0]

    def dma_group(self, eng, fns, sem, deps=()):
        gate = Tok(sem, self.cnt[sem]) if self.cnt[sem] > 0 else None
        tok = None
        for i, fn in enumerate(fns):
            d = (list(deps) + [gate]) if i == 0 else ()
            tok = self.step(eng, fn, deps=d, dma=True, sem=sem)
        return tok

    def build(self, nc):
        with ExitStack() as stack:
            sems = {}
            for name in list(ENGINES) + self.extra_sems:
                sems[name] = stack.enter_context(nc.semaphore(name))
            block = stack.enter_context(nc.Block())

            def emit_for(eng_name, e):
                for seng, fn, waits, sem_name, inc in self.steps:
                    if seng != eng_name:
                        continue
                    for (wsem, wval) in waits:
                        e.wait_ge(sems[wsem], wval)
                    fn(e).then_inc(sems[sem_name], inc)

            @block.sync
            def _(sync):
                emit_for("sync", sync)

            @block.vector
            def _(vector):
                emit_for("vector", vector)

            @block.scalar
            def _(scalar):
                emit_for("scalar", scalar)

            @block.tensor
            def _(tensor):
                emit_for("tensor", tensor)

            @block.gpsimd
            def _(gpsimd):
                emit_for("gpsimd", gpsimd)


class Buf:

    def __init__(self, ap, name="", init=None):
        self.ap = ap
        self.name = name
        self.w = None
        self.readers = list(init) if init else []

    def rdeps(self):
        return [self.w] if self.w is not None else []

    def wdeps(self):
        d = list(self.readers)
        if self.w is not None:
            d.append(self.w)
        return d

    def wrote(self, tok):
        self.w = tok
        self.readers = []

    def read(self, tok):
        self.readers.append(tok)


class Ctx:
    pass


def emit_ln(C):
    p = C.p
    stats, sttmp, xt, xnt = C.stats, C.sttmp, C.xt, C.xnt

    toks = []
    for c in range(NTC):
        xs = xt.ap[:, 1024 * c:1024 * (c + 1)]
        d0 = xt.rdeps() + sttmp.wdeps()
        p.step("vector", lambda e, xs=xs: e.bn_stats(sttmp.ap[:, 0:6], xs[:, 0:512]), deps=d0)
        t2 = p.step("vector", lambda e, xs=xs: e.bn_stats(sttmp.ap[:, 6:12], xs[:, 512:1024]), deps=d0)
        sttmp.wrote(t2)
        dst = stats.ap[:, c:c + 17:16]
        t3 = p.step("vector", lambda e, dst=dst: e.bn_aggr(dst, sttmp.ap[:, 0:12]),
                    deps=sttmp.rdeps() + stats.wdeps())
        sttmp.read(t3)
        xt.read(t3)
        toks.append(t3)
    stats.wrote(toks[-1])

    var = stats.ap[:, 16:32]
    vpe = stats.ap[:, 64:80]
    t = p.step("vector", lambda e: e.tensor_scalar_add(vpe, var, 1e-5), deps=toks)
    t2 = p.step("scalar", lambda e: e.activation(stats.ap[:, 32:48], vpe, AF.Sqrt), deps=[t])
    t3 = p.step("vector", lambda e: e.reciprocal(stats.ap[:, 32:48], stats.ap[:, 32:48]), deps=[t2])
    t4 = p.step("vector", lambda e: e.tensor_tensor(stats.ap[:, 48:64], stats.ap[:, 32:48],
                                                    stats.ap[:, 32:48], ALU.mult), deps=[t3])
    t5 = p.step("vector", lambda e: e.tensor_tensor(stats.ap[:, 48:64], stats.ap[:, 48:64], var, ALU.mult), deps=[t4])
    t6 = p.step("vector", lambda e: e.tensor_scalar_mul(stats.ap[:, 48:64], stats.ap[:, 48:64], 1024.0), deps=[t5])
    stats.wrote(t6)

    for c in range(NTC):
        xs = xt.ap[:, 1024 * c:1024 * (c + 1)]
        dst = xnt.ap[:, 1024 * c:1024 * (c + 1)]
        tok = p.step("vector",
                     lambda e, xs=xs, dst=dst, c=c: e.tensor_scalar(
                         dst, xs, C.stats.ap[:, c:c + 1], C.stats.ap[:, 32 + c:33 + c],
                         ALU.subtract, ALU.mult),
                     deps=stats.rdeps() + xt.rdeps() + xnt.wdeps())
        xnt.wrote(tok)
        stats.read(tok)
        xt.read(tok)


def emit_transpose_xn(C):
    p = C.p
    xnt, xnf = C.xnt, C.xnf
    banks = [C.pb[i] for i in (4, 5, 6, 7)]
    i = 0
    for c in range(NTC):
        for dc in range(NDC):
            bk = banks[i % 4]
            bkv = bk.bf16[:, 0:128]
            src = xnt.ap[:, 1024 * c + 128 * dc: 1024 * c + 128 * (dc + 1)]
            dst = xnf.ap[:, 2048 * dc + 128 * c: 2048 * dc + 128 * (c + 1)]
            tt = p.step("tensor", lambda e, bkv=bkv, src=src: e.transpose(bkv, src, C.ident_bf.ap),
                        deps=xnt.rdeps() + bk.buf.wdeps())
            bk.buf.wrote(tt)
            xnt.read(tt)
            te = p.step("scalar", lambda e, dst=dst, bkv=bkv: e.copy(dst, bkv),
                        deps=[tt] + xnf.wdeps())
            bk.buf.read(te)
            xnf.wrote(te)
            i += 1


def emit_l0_attn(C, b):
    p = C.p
    dram = C.dram
    H = 16

    ep = C.epoch
    o = 0
    WQKV = Buf(C.carve(o, 24 * 1024), "WQKV", ep); o += 24 * 1024
    WOUT = Buf(C.carve(o, 8 * 1024), "WOUT", ep); o += 8 * 1024
    QF = Buf(C.carve(o, 8 * 256), "QF", ep); o += 8 * 256
    KF = Buf(C.carve(o, 8 * 256), "KF", ep); o += 8 * 256
    KTV = Buf(C.carve(o, 2 * 1024), "KTV", ep); o += 2 * 1024
    VTV = Buf(C.carve(o, 2 * 1024), "VTV", ep); o += 2 * 1024
    PT = Buf(C.carve(o, 2 * 256), "PT", ep); o += 2 * 256
    NUM = Buf(C.carve(o, 256), "NUM", ep); o += 256
    SFS = Buf(C.carve(o, 2 * 512), "SFS", ep); o += 2 * 512
    SFZ = Buf(C.carve(o, 2 * 16), "SFZ", ep); o += 2 * 16
    SGS = Buf(C.carve(o, 512), "SGS", ep); o += 512
    SGZ = Buf(C.carve(o, 16), "SGZ", ep); o += 16
    AOF = Buf(C.carve(o, 8 * 256), "AOF", ep); o += 8 * 256
    FT = Buf(C.carve(o, 2 * 512), "FT", ep); o += 2 * 512
    DR = Buf(C.carve(o, 2 * 512), "DR", ep); o += 2 * 512
    assert o <= C.RSV0
    ftmp = FT.ap.bitcast(F32)
    drow = DR.ap.bitcast(F32)

    wsem = p.new_sem(f"l0aw_{b}")

    fns = []
    for c in range(NDC):
        srcq = dram["L0_Wqkv_t"][128 * c:128 * (c + 1), :]
        dstq = WQKV.ap[:, 3072 * c:3072 * (c + 1)]
        fns.append(lambda e, d=dstq, s=srcq: e.dma_start(out=d, in_=s))
        srco = dram["L0_Wout_t"][128 * c:128 * (c + 1), :]
        dsto = WOUT.ap[:, 1024 * c:1024 * (c + 1)]
        fns.append(lambda e, d=dsto, s=srco: e.dma_start(out=d, in_=s))
    wtok = p.dma_group("gpsimd", fns, wsem, deps=WQKV.wdeps() + WOUT.wdeps())
    WQKV.wrote(wtok)
    WOUT.wrote(wtok)

    SDB = C.pb[0]
    ZDB = [C.pb[1], C.pb[6]]
    PA = [C.pb[2], C.pb[3]]
    PO = C.pb[4]
    PDB = C.pb[5]
    PW = C.pb[7]
    sfsf = SFS.ap.bitcast(F32)
    sfzf = SFZ.ap.bitcast(F32)
    tz0 = p.step("vector", lambda e: e.memset(sfsf, 0.0), deps=SFS.wdeps())
    SFS.wrote(tz0)
    tz1 = p.step("vector", lambda e: e.memset(sfzf, 0.0), deps=SFZ.wdeps())
    SFZ.wrote(tz1)

    def elu1_evac(psum_ap, dst_ap, n, deps):
        t1 = p.step("vector", lambda e: e.tensor_scalar_min(ftmp[:, 0:n], psum_ap, 0.0),
                    deps=deps + FT.wdeps())
        t2 = p.step("scalar", lambda e: e.activation(ftmp[:, 0:n], ftmp[:, 0:n], AF.Exp), deps=[t1])
        FT.wrote(t2)
        t3 = p.step("vector", lambda e: e.scalar_tensor_tensor(dst_ap, psum_ap, 0.0, ftmp[:, 0:n],
                                                               ALU.max, ALU.add), deps=[t2])
        FT.read(t3)
        return t3

    def hslice(buf, h, c0, n):
        r = h // 2
        return buf.ap[64 * (h % 2):64 * (h % 2) + 64, 256 * r + c0: 256 * r + c0 + n]

    for chunk in range(8):
        t0 = 256 * chunk
        for m in range(16):
            mt = None
            deps0 = WQKV.rdeps() + C.xnf.rdeps() + PW.buf.wdeps()
            for kc in range(NDC):
                lhs = WQKV.ap[:, 3072 * kc + 128 * m: 3072 * kc + 128 * (m + 1)]
                rhs = C.xnf.ap[:, 2048 * kc + t0: 2048 * kc + t0 + 256]
                mt = p.step("tensor", lambda e, lhs=lhs, rhs=rhs, kc=kc: e.matmul(
                    PW.f32[:, 0:256], lhs, rhs, start=(kc == 0), stop=(kc == NDC - 1)),
                    deps=deps0 if kc == 0 else ())
            PW.buf.wrote(mt)
            WQKV.read(mt)
            C.xnf.read(mt)
            dstb = QF if m < 8 else KF
            dst = dstb.ap[:, 256 * (m % 8): 256 * (m % 8 + 1)]
            te = elu1_evac(PW.f32[:, 0:256], dst, 256, [mt] + dstb.wdeps())
            dstb.wrote(te)
            PW.buf.read(te)
        for s in range(2):
            for half in range(4):
                colbase = 1024 + 512 * half
                dbuf = KTV if half < 2 else VTV
                coloff = 512 * (half % 2)
                mt = None
                deps0 = C.xnf.rdeps() + WQKV.rdeps() + PW.buf.wdeps()
                for kc in range(NDC):
                    lhs = C.xnf.ap[:, 2048 * kc + t0 + 128 * s: 2048 * kc + t0 + 128 * (s + 1)]
                    rhs = WQKV.ap[:, 3072 * kc + colbase: 3072 * kc + colbase + 512]
                    mt = p.step("tensor", lambda e, lhs=lhs, rhs=rhs, kc=kc: e.matmul(
                        PW.f32, lhs, rhs, start=(kc == 0), stop=(kc == NDC - 1)),
                        deps=deps0 if kc == 0 else ())
                PW.buf.wrote(mt)
                C.xnf.read(mt)
                WQKV.read(mt)
                dst = dbuf.ap[:, 1024 * s + coloff: 1024 * s + coloff + 512]
                if half < 2:
                    te = elu1_evac(PW.f32, dst, 512, [mt] + dbuf.wdeps())
                else:
                    te = p.step("scalar", lambda e, dst=dst: e.copy(dst, PW.f32),
                                deps=[mt] + dbuf.wdeps())
                dbuf.wrote(te)
                PW.buf.read(te)

        for h in range(H):
            for ms in range(2):
                pa = PA[ms]
                lhs = hslice(KF, h, 128 * ms, 128)
                rhs = hslice(QF, h, 0, 256)
                tmm = p.step("tensor", lambda e, pa=pa, lhs=lhs, rhs=rhs: e.matmul(
                    pa.f32[:, 0:256], lhs, rhs, start=True, stop=True),
                    deps=KF.rdeps() + QF.rdeps() + pa.buf.wdeps())
                pa.buf.wrote(tmm)
                KF.read(tmm)
                QF.read(tmm)
                mask = (C.mask0 if ms == 0 else C.mask1)
                dst = PT.ap[:, 256 * ms:256 * (ms + 1)]
                tmk = p.step("vector", lambda e, dst=dst, pa=pa, mask=mask: e.scalar_tensor_tensor(
                    dst, pa.f32[:, 0:256], 0.0, mask.ap, ALU.add, ALU.mult),
                    deps=[tmm] + PT.wdeps() + mask.rdeps())
                PT.wrote(tmk)
                pa.buf.read(tmk)
            po_deps = PO.buf.wdeps() + VTV.rdeps() + PT.rdeps()
            pd_deps = PDB.buf.wdeps() + PT.rdeps() + C.ones_col_bf.rdeps()
            mm_n = None
            mm_d = None
            for ms in range(2):
                vsl = VTV.ap[:, 1024 * ms + 64 * h: 1024 * ms + 64 * (h + 1)]
                psl = PT.ap[:, 256 * ms:256 * (ms + 1)]
                last = (chunk == 0 and ms == 1)
                mm_n = p.step("tensor", lambda e, vsl=vsl, psl=psl, ms=ms, last=last: e.matmul(
                    PO.f32[0:64, 0:256], vsl, psl, start=(ms == 0), stop=last),
                    deps=po_deps if ms == 0 else ())
                VTV.read(mm_n)
                PT.read(mm_n)
            for ms in range(2):
                psl = PT.ap[:, 256 * ms:256 * (ms + 1)]
                last = (chunk == 0 and ms == 1)
                mm_d = p.step("tensor", lambda e, psl=psl, ms=ms, last=last: e.matmul(
                    PDB.f32[0:1, 0:256], C.ones_col_bf.ap, psl, start=(ms == 0), stop=last),
                    deps=pd_deps if ms == 0 else ())
                PT.read(mm_d)
            if chunk > 0:
                rb = 64 * (h % 2)
                cc = 64 * (h // 2)
                ssl = SGS.ap[rb:rb + 64, cc:cc + 64]
                zsl = SGZ.ap[rb:rb + 64, h:h + 1]
                qsl = hslice(QF, h, 0, 256)
                mm_n = p.step("tensor", lambda e, ssl=ssl, qsl=qsl: e.matmul(
                    PO.f32[0:64, 0:256], ssl, qsl, start=False, stop=True),
                    deps=SGS.rdeps())
                SGS.read(mm_n)
                QF.read(mm_n)
                mm_d = p.step("tensor", lambda e, zsl=zsl, qsl=qsl: e.matmul(
                    PDB.f32[0:1, 0:256], zsl, qsl, start=False, stop=True),
                    deps=SGZ.rdeps())
                SGZ.read(mm_d)
                QF.read(mm_d)
            PO.buf.wrote(mm_n)
            PDB.buf.wrote(mm_d)
            tl = p.step("scalar", lambda e: e.activation(drow[0:1, 0:256], PDB.f32[0:1, 0:256], AF.Ln),
                        deps=[mm_d] + DR.wdeps())
            PDB.buf.read(tl)
            tr = p.step("scalar", lambda e: e.activation(drow[0:1, 256:512], drow[0:1, 0:256],
                                                         AF.Exp, scale=-1.0), deps=[tl])
            DR.wrote(tr)
            tb = p.step("tensor", lambda e: e.matmul(PDB.f32[0:64, 256:512],
                                                     C.ones_row_f.ap[0:1, 0:64],
                                                     drow[0:1, 256:512], start=True, stop=True),
                        deps=[tr] + PDB.buf.wdeps() + C.ones_row_f.rdeps())
            PDB.buf.wrote(tb)
            DR.read(tb)
            tn = p.step("scalar", lambda e: e.copy(NUM.ap[0:64, 0:256], PO.f32[0:64, 0:256]),
                        deps=[mm_n] + NUM.wdeps())
            NUM.wrote(tn)
            PO.buf.read(tn)
            dsta = AOF.ap[64 * (h % 2):64 * (h % 2) + 64, 256 * (h // 2):256 * (h // 2) + 256]
            tc2 = p.step("vector", lambda e, dsta=dsta: e.tensor_tensor(
                dsta, NUM.ap[0:64, 0:256], PDB.f32[0:64, 256:512], ALU.mult),
                deps=[tn, tb] + AOF.wdeps())
            AOF.wrote(tc2)
            NUM.read(tc2)
            PDB.buf.read(tc2)
            if chunk < 7:
                rb = 64 * (h % 2)
                cc = 64 * (h // 2)
                for ms in range(2):
                    ksl = KTV.ap[:, 1024 * ms + 64 * h: 1024 * ms + 64 * (h + 1)]
                    vsl = VTV.ap[:, 1024 * ms + 64 * h: 1024 * ms + 64 * (h + 1)]
                    st = (ms == 0)
                    sp = (ms == 1)
                    mm_s = p.step("tensor", lambda e, ksl=ksl, vsl=vsl, st=st, sp=sp, rb=rb, cc=cc: e.matmul(
                        SDB.f32[rb:rb + 64, cc:cc + 64], ksl, vsl, start=st, stop=sp),
                        deps=SDB.buf.wdeps() if st else ())
                    SDB.buf.wrote(mm_s)
                    KTV.read(mm_s)
                    VTV.read(mm_s)
                    for rb2 in (0, 64):
                        zb = ZDB[rb2 // 64]
                        stz = (ms == 0)
                        spz = (ms == 1)
                        mm_z = p.step("tensor", lambda e, zb=zb, ksl=ksl, stz=stz, spz=spz, rb2=rb2, h=h: e.matmul(
                            zb.f32[rb2:rb2 + 64, h:h + 1], ksl, C.ones_col_bf.ap, start=stz, stop=spz),
                            deps=zb.buf.wdeps() if stz else ())
                        zb.buf.wrote(mm_z)
                        KTV.read(mm_z)
        if chunk < 7:
            tf1 = p.step("vector", lambda e: e.tensor_tensor(sfsf, sfsf, SDB.f32, ALU.add),
                         deps=SDB.buf.rdeps() + SFS.wdeps() + SFS.rdeps())
            SFS.wrote(tf1)
            SDB.buf.read(tf1)
            tf2a = p.step("vector", lambda e: e.tensor_tensor(
                sfzf[0:64, 0:16], sfzf[0:64, 0:16], ZDB[0].f32[0:64, 0:16], ALU.add),
                deps=ZDB[0].buf.rdeps() + SFZ.wdeps() + SFZ.rdeps())
            ZDB[0].buf.read(tf2a)
            tf2 = p.step("vector", lambda e: e.tensor_tensor(
                sfzf[64:128, 0:16], sfzf[64:128, 0:16], ZDB[1].f32[64:128, 0:16], ALU.add),
                deps=ZDB[1].buf.rdeps() + [tf2a] + SFZ.rdeps())
            SFZ.wrote(tf2)
            ZDB[1].buf.read(tf2)
            tg1 = p.step("vector", lambda e: e.tensor_copy(SGS.ap, sfsf), deps=[tf1] + SGS.wdeps())
            SGS.wrote(tg1)
            SFS.read(tg1)
            tg2 = p.step("vector", lambda e: e.tensor_copy(SGZ.ap, sfzf), deps=[tf2] + SGZ.wdeps())
            SGZ.wrote(tg2)
            SFZ.read(tg2)
        for ms in range(2):
            tc_g = 2 * chunk + ms
            for nh in range(2):
                mt = None
                deps0 = AOF.rdeps() + WOUT.rdeps() + PW.buf.wdeps()
                for r in range(NDC):
                    lhs = AOF.ap[:, 256 * r + 128 * ms: 256 * r + 128 * (ms + 1)]
                    rhs = WOUT.ap[:, 1024 * r + 512 * nh: 1024 * r + 512 * (nh + 1)]
                    mt = p.step("tensor", lambda e, lhs=lhs, rhs=rhs, r=r: e.matmul(
                        PW.f32, lhs, rhs, start=(r == 0), stop=(r == NDC - 1)),
                        deps=deps0 if r == 0 else ())
                PW.buf.wrote(mt)
                AOF.read(mt)
                WOUT.read(mt)
                xsl = C.xt.ap[:, 1024 * tc_g + 512 * nh: 1024 * tc_g + 512 * (nh + 1)]
                ta = p.step("vector", lambda e, xsl=xsl: e.tensor_tensor(xsl, xsl, PW.f32, ALU.add),
                            deps=[mt] + C.xt.wdeps())
                C.xt.wrote(ta)
                PW.buf.read(ta)

def emit_l0_routing(C, b):
    p = C.p
    dram = C.dram
    ep = C.epoch
    o = 0
    CEN = Buf(C.carve(o, 2 * 64), "CEN", ep); o += 2 * 64
    MISC = Buf(C.carve(o, 2 * 64), "MISC", ep); o += 2 * 64
    XNTF = Buf(C.carve(o, 2 * 1024), "XNTF", ep); o += 2 * 1024
    XNFF = Buf(C.carve(o, 2 * 1024), "XNFF", ep); o += 2 * 1024
    CBF = Buf(C.carve(o, 2 * 64), "CBF", ep); o += 2 * 64
    CSF = Buf(C.carve(o, 2 * 64), "CSF", ep); o += 2 * 64
    r = C.RSV0
    WE = Buf(C.carve(r, 2 * 128), "WE", ep); r += 2 * 128
    WET = Buf(C.carve(r, 2048), "WET", ep); r += 2048
    C.l0_we = WE
    C.l0_wet = WET
    cenf = CEN.ap.bitcast(F32)
    wef = WE.ap.bitcast(F32)
    miscf = MISC.ap.bitcast(F32)

    wsem = p.new_sem(f"l0rw_{b}")

    src = dram["L0_cent_t"].rearrange("(c p) e -> p c e", p=128)
    tok = p.step("sync", lambda e: e.dma_start(out=cenf.rearrange("p (c e) -> p c e", c=8), in_=src),
                 deps=CEN.wdeps(), dma=True, sem=wsem)
    CEN.wrote(tok)
    cbff = CBF.ap.bitcast(F32)
    csff = CSF.ap.bitcast(F32)
    t1 = p.step("vector", lambda e: e.tensor_scalar_mul(cbff, cenf, -2.0), deps=CEN.rdeps() + CBF.wdeps())
    CBF.wrote(t1)
    CEN.read(t1)
    t2 = p.step("scalar", lambda e: e.activation(csff, cenf, AF.Square), deps=CEN.rdeps() + CSF.wdeps())
    CSF.wrote(t2)
    CEN.read(t2)
    PR = C.pb[3]
    mm = None
    for kc in range(NDC):
        mm = p.step("tensor", lambda e, kc=kc: e.matmul(PR.f32[0:1, 0:8], C.ones_col_f.ap,
                                                        csff[:, 8 * kc:8 * (kc + 1)],
                                                        start=(kc == 0), stop=(kc == NDC - 1)),
                    deps=(CSF.rdeps() + PR.buf.wdeps() + C.ones_col_f.rdeps()) if kc == 0 else ())
        CSF.read(mm)
    PR.buf.wrote(mm)
    tc2r = p.step("scalar", lambda e: e.copy(miscf[0:1, 0:8], PR.f32[0:1, 0:8]),
                  deps=[mm] + MISC.wdeps())
    PR.buf.read(tc2r)
    MISC.wrote(tc2r)

    PD = C.pb[4]
    PTR = C.pb[6]
    xntff = XNTF.ap.bitcast(F32)
    xnfff = XNFF.ap.bitcast(F32)
    for tc in range(NTC):
        txn = p.step("vector", lambda e, tc=tc: e.tensor_scalar(
            xntff, C.xt.ap[:, 1024 * tc:1024 * (tc + 1)],
            C.stats.ap[:, tc:tc + 1], C.stats.ap[:, 32 + tc:33 + tc],
            ALU.subtract, ALU.mult),
            deps=C.stats.rdeps() + C.xt.rdeps() + XNTF.wdeps())
        XNTF.wrote(txn)
        C.stats.read(txn)
        C.xt.read(txn)
        tes = []
        for dc in range(NDC):
            tt_ = p.step("tensor", lambda e, dc=dc: e.transpose(
                PTR.f32[:, 0:128], xntff[:, 128 * dc:128 * (dc + 1)], C.ident_f32.ap),
                deps=[txn] + PTR.buf.wdeps() + C.ident_f32.rdeps())
            PTR.buf.wrote(tt_)
            XNTF.read(tt_)
            te_ = p.step("scalar", lambda e, dc=dc: e.copy(xnfff[:, 128 * dc:128 * (dc + 1)],
                                                           PTR.f32[:, 0:128]),
                         deps=[tt_] + (XNFF.wdeps() if dc == 0 else []))
            PTR.buf.read(te_)
            tes.append(te_)
        XNFF.wrote(tes[-1])
        mm = None
        deps0 = XNFF.rdeps() + [tes[-1]] + CBF.rdeps() + PD.buf.wdeps()
        for dc in range(NDC):
            lhs = xnfff[:, 128 * dc:128 * (dc + 1)]
            rhs = cbff[:, 8 * dc:8 * (dc + 1)]
            mm = p.step("tensor", lambda e, lhs=lhs, rhs=rhs, dc=dc: e.matmul(
                PD.f32[:, 0:8], lhs, rhs, start=(dc == 0), stop=False),
                deps=deps0 if dc == 0 else ())
            CBF.read(mm)
            XNFF.read(mm)
        mm = p.step("tensor", lambda e: e.matmul(PD.f32[:, 0:8], C.ones_row_f.ap,
                                                 miscf[0:1, 0:8], start=False, stop=True),
                    deps=MISC.rdeps())
        MISC.read(mm)
        PD.buf.wrote(mm)
        td1 = p.step("vector", lambda e, tc=tc: e.tensor_scalar(
            miscf[:, 8:16], PD.f32[:, 0:8], C.stats.ap[:, 48 + tc:49 + tc], 1e-12, ALU.add, ALU.max),
            deps=[mm] + MISC.wdeps() + C.stats.rdeps())
        PD.buf.read(td1)
        C.stats.read(td1)
        td2 = p.step("scalar", lambda e: e.activation(miscf[:, 8:16], miscf[:, 8:16], AF.Sqrt), deps=[td1])
        td3 = p.step("vector", lambda e: e.max(out=miscf[:, 16:24], in_=miscf[:, 8:16]), deps=[td2])
        td4 = p.step("vector", lambda e: e.tensor_scalar(miscf[:, 24:25], miscf[:, 16:17],
                                                         miscf[:, 17:18], None, ALU.subtract), deps=[td3])
        td5 = p.step("scalar", lambda e: e.activation(miscf[:, 25:26], miscf[:, 24:25], AF.Sigmoid), deps=[td4])
        td6 = p.step("vector", lambda e: e.tensor_scalar(miscf[:, 26:27], miscf[:, 25:26],
                                                         -1.0, 1.0, ALU.mult, ALU.add), deps=[td5])
        td7a = p.step("vector", lambda e: e.tensor_scalar(miscf[:, 48:56], miscf[:, 8:16],
                                                          miscf[:, 16:17], None,
                                                          ALU.is_equal), deps=[td3, td2])
        td7 = p.step("vector", lambda e: e.tensor_scalar(miscf[:, 32:40], miscf[:, 48:56],
                                                         miscf[:, 25:26], None,
                                                         ALU.mult), deps=[td7a, td5])
        td8a = p.step("vector", lambda e: e.tensor_scalar(miscf[:, 40:48], miscf[:, 8:16],
                                                          miscf[:, 17:18], miscf[:, 26:27],
                                                          ALU.is_equal, ALU.mult), deps=[td6])
        td8b = p.step("vector", lambda e: e.tensor_scalar(miscf[:, 56:64], miscf[:, 48:56],
                                                          -1.0, 1.0, ALU.mult, ALU.add), deps=[td7a])
        td8 = p.step("vector", lambda e: e.tensor_tensor(miscf[:, 40:48], miscf[:, 40:48],
                                                         miscf[:, 56:64], ALU.mult), deps=[td8a, td8b])
        td9 = p.step("vector", lambda e, tc=tc: e.tensor_tensor(
            wef[:, 8 * tc:8 * (tc + 1)], miscf[:, 32:40], miscf[:, 40:48], ALU.add),
            deps=[td7, td8] + WE.wdeps())
        MISC.wrote(td8)
        WE.wrote(td9)
        MISC.read(td9)
        PTT = C.pb[5]
        tt = p.step("tensor", lambda e, tc=tc: e.transpose(
            PTT.f32[0:8, 0:128], wef[:, 8 * tc:8 * (tc + 1)], C.ident_f32.ap),
            deps=[td9] + PTT.buf.wdeps())
        PTT.buf.wrote(tt)
        WE.read(tt)
        tt2 = p.step("scalar", lambda e, tc=tc: e.copy(WET.ap[0:8, 128 * tc:128 * (tc + 1)],
                                                       PTT.f32[0:8, 0:128]),
                     deps=[tt] + WET.wdeps())
        PTT.buf.read(tt2)
        WET.wrote(tt2)


def emit_l0_ffn(C, b, h_dram):
    p = C.p
    dram = C.dram
    HID = 2736
    NHC = 22
    QCHUNKS = [(0, 6), (6, 6), (12, 6), (18, 4)]

    ep = C.epoch
    WE = C.l0_we
    WET = C.l0_wet
    wef = WE.ap.bitcast(F32)
    o = 0
    SLOT_E = 6 * 1024
    SLOTS = []
    for i in range(6):
        SLOTS.append(Buf(C.carve(o, SLOT_E), f"SLOT{i}", ep)); o += SLOT_E
    HACC = Buf(C.carve(o, 6 * 512), "HACC", ep); o += 6 * 512
    HB = [Buf(C.carve(o, 22 * 128), "HB0", ep), None]; o += 22 * 128
    HB[1] = Buf(C.carve(o, 22 * 128), "HB1", ep); o += 22 * 128
    GST = Buf(C.carve(o, 512), "GST", ep); o += 512
    B2B = Buf(C.carve(o, 1024), "B2B", ep); o += 1024
    BGC = Buf(C.carve(o, 2 * 176), "BGC", ep); o += 2 * 176
    B1C = Buf(C.carve(o, 2 * 176), "B1C", ep); o += 2 * 176
    assert o <= C.RSV0, o
    bgc = BGC.ap.bitcast(F32)
    b1c = B1C.ap.bitcast(F32)

    hsp = p.new_sem(f"l0sp_{b}")
    hbsem = [p.new_sem(f"l0hb{i}_{b}") for i in range(2)]
    slotsem = [p.new_sem(f"l0sl{i}_{b}") for i in range(6)]
    biassem = p.new_sem(f"l0bias_{b}")

    bfns = []
    for name, bufv, apv in (("L0_bg_t", BGC, bgc), ("L0_b1_t", B1C, b1c)):
        srcb = dram[name].rearrange("(hc p) e -> p hc e", p=128)
        bfns.append(lambda e, apv=apv, srcb=srcb: e.dma_start(
            out=apv.rearrange("p (hc e) -> p hc e", hc=22), in_=srcb))
    tokb = p.dma_group("sync", bfns, biassem, deps=BGC.wdeps() + B1C.wdeps())
    BGC.wrote(tokb)
    B1C.wrote(tokb)
    b2sem = p.new_sem(f"l0b2_{b}")
    tokb2 = p.step("gpsimd", lambda e: e.dma_start(out=B2B.ap[0:8, :], in_=dram["L0_b2"]),
                   deps=B2B.wdeps(), dma=True, sem=b2sem)
    B2B.wrote(tokb2)

    def load_q(slot, slot_i, src_ap, nslabs, slab_rows, slab_cols):
        fns = []
        for j in range(nslabs):
            dst = slot.ap[0:slab_rows[j], slab_cols * j: slab_cols * j + src_ap[j].shape[1]]
            fns.append(lambda e, d=dst, s=src_ap[j]: e.dma_start(out=d, in_=s))
        tok = p.dma_group("gpsimd", fns, slotsem[slot_i], deps=slot.wdeps())
        slot.wrote(tok)
        return tok

    def load_gh_quarter(slot, slot_i, wname, e_idx, q):
        hc0, nq = QCHUNKS[q]
        c0 = 128 * hc0
        c1 = min(HID, c0 + 128 * nq)
        srcs = [dram[wname][e_idx][128 * kc:128 * (kc + 1), c0:c1] for kc in range(NDC)]
        return load_q(slot, slot_i, srcs, NDC, [128] * NDC, 768)

    def load_w2_quarter(slot, slot_i, e_idx, q):
        hc0, nq = QCHUNKS[q]
        srcs = []
        rows = []
        for j in range(nq):
            r0 = 128 * (hc0 + j)
            r1 = min(HID, r0 + 128)
            srcs.append(dram["L0_W2"][e_idx][r0:r1, :])
            rows.append(r1 - r0)
        return load_q(slot, slot_i, srcs, nq, rows, 1024)

    tzp = p.step("vector", lambda e: e.memset(HB[0].ap[0:80, 0:2048], 0.0), deps=HB[0].wdeps())
    HB[0].wrote(tzp)
    tzd = p.step("sync", lambda e: e.dma_start(out=h_dram[2736:2816, :], in_=HB[0].ap[0:80, 0:2048]),
                 deps=[tzp], dma=True, sem=hbsem[0])
    HB[0].read(tzd)

    load_gh_quarter(SLOTS[0], 0, "L0_Wg", 0, 0)
    load_gh_quarter(SLOTS[1], 1, "L0_W1", 0, 0)

    PG = [C.pb[0], C.pb[1], C.pb[2], C.pb[3]]
    PH = [C.pb[4], C.pb[5], C.pb[6], C.pb[7]]

    for e_i in range(8):
        gslot = {0: (0, 1), 1: (2, 3), 2: (4, 5), 3: (0, 1)}
        w2slot = {0: 2, 1: 3, 2: 4, 3: 5}
        for q in range(4):
            hc0, nq = QCHUNKS[q]
            sg, s1 = (SLOTS[gslot[q][0]], SLOTS[gslot[q][1]])
            if q == 0:
                load_gh_quarter(SLOTS[2], 2, "L0_Wg", e_i, 1)
                load_gh_quarter(SLOTS[3], 3, "L0_W1", e_i, 1)
            elif q == 1:
                load_gh_quarter(SLOTS[4], 4, "L0_Wg", e_i, 2)
                load_gh_quarter(SLOTS[5], 5, "L0_W1", e_i, 2)
            elif q == 2:
                load_gh_quarter(SLOTS[0], 0, "L0_Wg", e_i, 3)
                load_gh_quarter(SLOTS[1], 1, "L0_W1", e_i, 3)
            elif q == 3:
                load_w2_quarter(SLOTS[2], 2, e_i, 0)
                load_w2_quarter(SLOTS[3], 3, e_i, 1)
                load_w2_quarter(SLOTS[4], 4, e_i, 2)
                load_w2_quarter(SLOTS[5], 5, e_i, 3)
            for hcl in range(nq):
                hcg = hc0 + hcl
                mh = min(128, HID - 128 * hcg)
                for w_i, (slot, pbanks) in enumerate(((sg, PG), (s1, PH))):
                    for kc in range(NDC):
                        lhs = slot.ap[:, 768 * kc + 128 * hcl: 768 * kc + 128 * hcl + mh]
                        for tt in range(4):
                            rhs = C.xnf.ap[:, 2048 * kc + 512 * tt: 2048 * kc + 512 * (tt + 1)]
                            pk = pbanks[tt]
                            mm = p.step("tensor", lambda e, pk=pk, lhs=lhs, rhs=rhs, kc=kc, mh=mh: e.matmul(
                                pk.f32[0:mh, :], lhs, rhs, start=(kc == 0), stop=(kc == NDC - 1)),
                                deps=(slot.rdeps() + C.xnf.rdeps() + pk.buf.wdeps()) if kc == 0 else ())
                            if kc == 0:
                                pk.buf.wrote(mm)
                            slot.read(mm)
                            C.xnf.read(mm)
                            if kc == NDC - 1:
                                pk.buf.wrote(mm)
                bcol = 8 * hcg + e_i
                for tt in range(4):
                    tg = p.step("scalar", lambda e, tt=tt, mh=mh, bcol=bcol: e.activation(
                        GST.ap[0:mh, :], PG[tt].f32[0:mh, :], AF.Silu, bias=bgc[0:mh, bcol:bcol + 1]),
                        deps=PG[tt].buf.rdeps() + GST.wdeps() + BGC.rdeps())
                    GST.wrote(tg)
                    PG[tt].buf.read(tg)
                    BGC.read(tg)
                    dst = HACC.ap[0:mh, 512 * hcl:512 * (hcl + 1)]
                    th = p.step("vector", lambda e, tt=tt, mh=mh, bcol=bcol, dst=dst: e.scalar_tensor_tensor(
                        dst, PH[tt].f32[0:mh, :], b1c[0:mh, bcol:bcol + 1], GST.ap[0:mh, :],
                        ALU.add, ALU.mult),
                        deps=[tg] + PH[tt].buf.rdeps() + HACC.wdeps() + B1C.rdeps())
                    HACC.wrote(th)
                    PH[tt].buf.read(th)
                    GST.read(th)
                    B1C.read(th)
                    hdst = h_dram[128 * hcg:128 * hcg + mh, 512 * tt:512 * (tt + 1)]
                    ts_ = p.step("sync", lambda e, hdst=hdst, dst=dst, mh=mh: e.dma_start(out=hdst, in_=dst),
                                 deps=[th], dma=True, sem=hsp)
                    HACC.read(ts_)
        PW2 = [C.pb[6], C.pb[7]]
        for tcg in range(NTC):
            if tcg == 8 and e_i < 7:
                load_gh_quarter(SLOTS[0], 0, "L0_Wg", e_i + 1, 0)
                load_gh_quarter(SLOTS[1], 1, "L0_W1", e_i + 1, 0)
            hb = HB[tcg % 2]
            hsrc = h_dram[0:2816, 128 * tcg:128 * (tcg + 1)].rearrange("(hc p) t -> p hc t", p=128)
            spill_done = Tok(hsp, p.cnt[hsp])
            tl = p.dma_group("sync", [lambda e, hb=hb, hsrc=hsrc: e.dma_start(
                out=hb.ap.rearrange("p (hc t) -> p hc t", hc=22), in_=hsrc)],
                hbsem[tcg % 2], deps=hb.wdeps() + [spill_done])
            hb.wrote(tl)
            mm = None
            for hcg in range(NHC):
                q = 0 if hcg < 6 else (1 if hcg < 12 else (2 if hcg < 18 else 3))
                hcl = hcg - QCHUNKS[q][0]
                slot = SLOTS[w2slot[q]]
                mh = min(128, HID - 128 * hcg)
                lhs = hb.ap[0:mh, 128 * hcg:128 * hcg + 128]
                for nh in range(2):
                    rhs = slot.ap[0:mh, 1024 * hcl + 512 * nh: 1024 * hcl + 512 * (nh + 1)]
                    pk = PW2[nh]
                    mm = p.step("tensor", lambda e, pk=pk, lhs=lhs, rhs=rhs, hcg=hcg, mh=mh: e.matmul(
                        pk.f32, lhs, rhs, start=(hcg == 0), stop=(hcg == NHC - 1)),
                        deps=(slot.rdeps() + [tl] + pk.buf.wdeps()) if hcg == 0 else slot.rdeps())
                    pk.buf.wrote(mm)
                    slot.read(mm)
                    hb.read(mm)
            for nh in range(2):
                xsl = C.xt.ap[:, 1024 * tcg + 512 * nh: 1024 * tcg + 512 * (nh + 1)]
                wecol = wef[:, 8 * tcg + e_i:8 * tcg + e_i + 1]
                ta = p.step("vector", lambda e, xsl=xsl, wecol=wecol, nh=nh: e.scalar_tensor_tensor(
                    xsl, PW2[nh].f32, wecol, xsl, ALU.mult, ALU.add),
                    deps=PW2[nh].buf.rdeps() + C.xt.wdeps() + WE.rdeps())
                C.xt.wrote(ta)
                PW2[nh].buf.read(ta)
                WE.read(ta)

    PB2 = C.pb[0]
    for tc in range(NTC):
        for nh in range(2):
            lhs = WET.ap[0:8, 128 * tc:128 * (tc + 1)]
            rhs = B2B.ap[0:8, 512 * nh:512 * (nh + 1)]
            mm = p.step("tensor", lambda e, lhs=lhs, rhs=rhs: e.matmul(
                PB2.f32, lhs, rhs, start=True, stop=True),
                deps=WET.rdeps() + B2B.rdeps() + PB2.buf.wdeps())
            PB2.buf.wrote(mm)
            WET.read(mm)
            B2B.read(mm)
            xsl = C.xt.ap[:, 1024 * tc + 512 * nh: 1024 * tc + 512 * (nh + 1)]
            ta = p.step("vector", lambda e, xsl=xsl: e.tensor_tensor(xsl, xsl, PB2.f32, ALU.add),
                        deps=[mm] + C.xt.wdeps())
            C.xt.wrote(ta)
            PB2.buf.read(ta)



def emit_l1(C, b, layers_seen=None):
    emit_l1_sel(C, b)
    C.epoch = [C.p.barrier_tok(e) for e in ENGINES]
    emit_l1_attn(C, b)
    C.epoch = [C.p.barrier_tok(e) for e in ENGINES]
    emit_l1_rails(C, b)


def emit_l1_sel(C, b):
    p = C.p
    dram = C.dram
    ep = C.epoch
    o = C.SCR0
    XNTF = Buf(C.carve(o, 2 * 1024), "XNTF1", ep); o += 2 * 1024
    XNFF = Buf(C.carve(o, 2 * 1024), "XNFF1", ep); o += 2 * 1024
    WIMP = Buf(C.carve(o, 2 * 8), "WIMP", ep); o += 2 * 8
    SCOL = Buf(C.carve(o, 2 * 16), "SCOL", ep); o += 2 * 16
    MSK = Buf(C.carve(o, 2 * 16), "MSK", ep); o += 2 * 16
    RNK = Buf(C.carve(o, 2 * 16), "RNK", ep); o += 2 * 16
    BTMP = Buf(C.carve(o, 2 * 320), "BTMP", ep); o += 2 * 320
    TTMP = Buf(C.carve(o, 2 * 128), "TTMP", ep); o += 2 * 128
    PALL = Buf(C.carve(o, 16 * 256), "PALL", ep); o += 16 * 256
    assert o <= C.RSV0
    r = C.RSV0
    SELF = Buf(C.carve(r, 8 * 256), "SELF", ep); r += 8 * 256
    C.l1_self = SELF
    xntff = XNTF.ap.bitcast(F32)
    xnfff = XNFF.ap.bitcast(F32)
    wimpf = WIMP.ap.bitcast(F32)
    scolf = SCOL.ap.bitcast(F32)
    mskf = MSK.ap.bitcast(F32)
    rnkf = RNK.ap.bitcast(F32)
    btf = BTMP.ap.bitcast(F32)
    ttf = TTMP.ap.bitcast(F32)

    wsem = p.new_sem(f"l1sw_{b}")
    tok = p.step("sync", lambda e: e.dma_start(out=wimpf, in_=dram["L1_wimp"]),
                 deps=WIMP.wdeps(), dma=True, sem=wsem)
    WIMP.wrote(tok)

    PTR = C.pb[0]
    PS = C.pb[1]
    for tc in range(NTC):
        txn = p.step("vector", lambda e, tc=tc: e.tensor_scalar(
            xntff, C.xt.ap[:, 1024 * tc:1024 * (tc + 1)],
            C.stats.ap[:, tc:tc + 1], C.stats.ap[:, 32 + tc:33 + tc],
            ALU.subtract, ALU.mult),
            deps=C.stats.rdeps() + C.xt.rdeps() + XNTF.wdeps())
        XNTF.wrote(txn)
        C.stats.read(txn)
        C.xt.read(txn)
        tes = []
        for dc in range(NDC):
            tt_ = p.step("tensor", lambda e, dc=dc: e.transpose(
                PTR.f32[:, 0:128], xntff[:, 128 * dc:128 * (dc + 1)], C.ident_f32.ap),
                deps=[txn] + PTR.buf.wdeps() + C.ident_f32.rdeps())
            PTR.buf.wrote(tt_)
            XNTF.read(tt_)
            te_ = p.step("scalar", lambda e, dc=dc: e.copy(xnfff[:, 128 * dc:128 * (dc + 1)],
                                                           PTR.f32[:, 0:128]),
                         deps=[tt_] + (XNFF.wdeps() if dc == 0 else []))
            PTR.buf.read(te_)
            tes.append(te_)
        XNFF.wrote(tes[-1])
        mm = None
        for dc in range(NDC):
            mm = p.step("tensor", lambda e, dc=dc: e.matmul(
                PS.f32[:, 0:1], xnfff[:, 128 * dc:128 * (dc + 1)], wimpf[:, dc:dc + 1],
                start=(dc == 0), stop=(dc == NDC - 1)),
                deps=(XNFF.rdeps() + [tes[-1]] + WIMP.rdeps() + PS.buf.wdeps()) if dc == 0 else ())
            WIMP.read(mm)
            XNFF.read(mm)
        PS.buf.wrote(mm)
        tev = p.step("vector", lambda e, tc=tc: e.tensor_copy(scolf[:, tc:tc + 1], PS.f32[:, 0:1]),
                     deps=[mm] + (SCOL.wdeps() if tc == 0 else []))
        PS.buf.read(tev)
    SCOL.wrote(p.barrier_tok("vector"))

    t1 = p.step("tensor", lambda e: e.transpose(PTR.f32[0:16, 0:128], scolf[:, 0:16], C.ident_f32.ap),
                deps=SCOL.rdeps() + PTR.buf.wdeps())
    PTR.buf.wrote(t1)
    SCOL.read(t1)
    t2 = p.step("scalar", lambda e: e.copy(ttf[0:16, 0:128], PTR.f32[0:16, 0:128]),
                deps=[t1] + TTMP.wdeps())
    PTR.buf.read(t2)
    TTMP.wrote(t2)
    t3 = p.step("vector", lambda e: e.reduce_max(btf[0:16, 16:17], ttf[0:16, 0:128], AX.X), deps=[t2])
    t5 = p.step("vector", lambda e: e.tensor_scalar_mul(ttf[0:16, 0:128], ttf[0:16, 0:128], -1.0), deps=[t3, t2])
    t6 = p.step("vector", lambda e: e.reduce_max(btf[0:16, 18:19], ttf[0:16, 0:128], AX.X), deps=[t5])
    TTMP.wrote(t6)
    t7 = p.step("tensor", lambda e: e.transpose(PTR.f32[0:1, 0:16], btf[0:16, 16:17], C.ident_f32.ap[0:16, 0:16]),
                deps=[t3, t6] + PTR.buf.wdeps())
    PTR.buf.wrote(t7)
    t7b = p.step("tensor", lambda e: e.transpose(PTR.f32[0:1, 16:32], btf[0:16, 18:19], C.ident_f32.ap[0:16, 0:16]),
                 deps=[t3, t6])
    PTR.buf.wrote(t7b)
    t8 = p.step("scalar", lambda e: e.copy(ttf[0:1, 0:32], PTR.f32[0:1, 0:32]), deps=[t7, t7b] + TTMP.wdeps())
    PTR.buf.read(t8)
    TTMP.wrote(t8)
    t9 = p.step("vector", lambda e: e.reduce_max(btf[0:1, 1:2], ttf[0:1, 0:16], AX.X), deps=[t8])
    t10 = p.step("vector", lambda e: e.reduce_max(btf[0:1, 5:6], ttf[0:1, 16:32], AX.X), deps=[t8])
    t11 = p.step("vector", lambda e: e.tensor_scalar(btf[0:1, 0:1], btf[0:1, 5:6], -1.0, -1.0,
                                                     ALU.mult, ALU.add), deps=[t10])
    BTMP.wrote(t11)
    lastb = [t9, t11]
    for it in range(42):
        m1 = p.step("vector", lambda e: e.tensor_tensor(btf[0:1, 2:3], btf[0:1, 0:1], btf[0:1, 1:2], ALU.add),
                    deps=lastb + BTMP.wdeps())
        m2 = p.step("vector", lambda e: e.tensor_scalar_mul(btf[0:1, 2:3], btf[0:1, 2:3], 0.5), deps=[m1])
        m3 = p.step("tensor", lambda e: e.matmul(PS.f32[:, 1:2], C.ones_row_f.ap, btf[0:1, 2:3],
                                                 start=True, stop=True),
                    deps=[m2] + PS.buf.wdeps() + C.ones_row_f.rdeps())
        PS.buf.wrote(m3)
        m4 = p.step("vector", lambda e: e.tensor_scalar(btf[:, 8:24], scolf[:, 0:16], PS.f32[:, 1:2],
                                                        None, ALU.is_gt),
                    deps=[m3] + SCOL.rdeps())
        PS.buf.read(m4)
        m5 = p.step("vector", lambda e: e.reduce_sum(btf[:, 3:4], btf[:, 8:24], AX.X), deps=[m4])
        m6 = p.step("tensor", lambda e: e.matmul(PS.f32[0:1, 2:3], btf[:, 3:4], C.ones_col_f.ap,
                                                 start=True, stop=True),
                    deps=[m5] + PS.buf.wdeps() + C.ones_col_f.rdeps())
        PS.buf.wrote(m6)
        m7 = p.step("vector", lambda e: e.tensor_scalar(btf[0:1, 4:5], PS.f32[0:1, 2:3], 256.0, None,
                                                        ALU.is_ge), deps=[m6])
        PS.buf.read(m7)
        m8 = p.step("vector", lambda e: e.tensor_tensor(btf[0:1, 5:6], btf[0:1, 2:3], btf[0:1, 0:1], ALU.subtract),
                    deps=[m7, m2])
        m9 = p.step("vector", lambda e: e.scalar_tensor_tensor(btf[0:1, 5:6], btf[0:1, 5:6], 1.0,
                                                               btf[0:1, 4:5], ALU.mult, ALU.mult), deps=[m8])
        m10 = p.step("vector", lambda e: e.tensor_tensor(btf[0:1, 0:1], btf[0:1, 0:1], btf[0:1, 5:6], ALU.add),
                     deps=[m9])
        m11 = p.step("vector", lambda e: e.tensor_tensor(btf[0:1, 6:7], btf[0:1, 1:2], btf[0:1, 2:3], ALU.subtract),
                     deps=[m7, m2])
        m12 = p.step("vector", lambda e: e.scalar_tensor_tensor(btf[0:1, 6:7], btf[0:1, 6:7], 1.0,
                                                                btf[0:1, 4:5], ALU.mult, ALU.mult), deps=[m11])
        m13 = p.step("vector", lambda e: e.scalar_tensor_tensor(btf[0:1, 1:2], btf[0:1, 2:3], 1.0,
                                                                btf[0:1, 6:7], ALU.mult, ALU.add), deps=[m12, m10])
        BTMP.wrote(m13)
        lastb = [m13]
    mb = p.step("tensor", lambda e: e.matmul(PS.f32[:, 1:2], C.ones_row_f.ap, btf[0:1, 0:1],
                                             start=True, stop=True),
                deps=lastb + PS.buf.wdeps())
    PS.buf.wrote(mb)
    mm1 = p.step("vector", lambda e: e.tensor_scalar(mskf[:, 0:16], scolf[:, 0:16], PS.f32[:, 1:2],
                                                     None, ALU.is_gt),
                 deps=[mb] + SCOL.rdeps() + MSK.wdeps())
    PS.buf.read(mm1)
    MSK.wrote(mm1)
    r1 = p.step("tensor", lambda e: e.transpose(PTR.f32[0:16, 0:128], mskf[:, 0:16], C.ident_f32.ap),
                deps=MSK.rdeps() + PTR.buf.wdeps())
    PTR.buf.wrote(r1)
    MSK.read(r1)
    r2 = p.step("scalar", lambda e: e.copy(ttf[0:16, 0:128], PTR.f32[0:16, 0:128]), deps=[r1] + TTMP.wdeps())
    PTR.buf.read(r2)
    TTMP.wrote(r2)
    r3 = p.step("vector", lambda e: e.memset(btf[0:16, 160:288], 0.0), deps=BTMP.wdeps())
    r4 = p.step("vector", lambda e: e.tensor_tensor_scan(btf[0:16, 32:160], ttf[0:16, 0:128],
                                                         btf[0:16, 160:288], 0.0, ALU.add, ALU.bypass),
                deps=[r2, r3])
    BTMP.wrote(r4)
    r5 = p.step("vector", lambda e: e.reduce_sum(btf[0:16, 24:25], ttf[0:16, 0:128], AX.X), deps=[r2])
    r6 = p.step("tensor", lambda e: e.matmul(PS.f32[0:16, 3:4], C.sut16.ap, btf[0:16, 24:25],
                                             start=True, stop=True),
                deps=[r5] + PS.buf.wdeps() + C.sut16.rdeps())
    PS.buf.wrote(r6)
    r7 = p.step("vector", lambda e: e.tensor_scalar(btf[0:16, 32:160], btf[0:16, 32:160], PS.f32[0:16, 3:4],
                                                    None, ALU.add), deps=[r4, r6])
    PS.buf.read(r7)
    r8 = p.step("vector", lambda e: e.tensor_tensor(btf[0:16, 32:160], btf[0:16, 32:160], ttf[0:16, 0:128],
                                                    ALU.subtract), deps=[r7])
    BTMP.wrote(r8)
    r9 = p.step("tensor", lambda e: e.transpose(PTR.f32[:, 0:16], btf[0:16, 32:160], C.ident_f32.ap[0:16, 0:16]),
                deps=[r8] + PTR.buf.wdeps())
    PTR.buf.wrote(r9)
    BTMP.read(r9)
    r10 = p.step("vector", lambda e: e.tensor_copy(rnkf[:, 0:16], PTR.f32[:, 0:16]),
                 deps=[r9] + RNK.wdeps())
    PTR.buf.read(r10)
    RNK.wrote(r10)
    for tc in range(NTC):
        tp = p.step("vector", lambda e, tc=tc: e.tensor_scalar(
            PALL.ap[:, 256 * tc:256 * (tc + 1)], C.iotat.ap, rnkf[:, tc:tc + 1], mskf[:, tc:tc + 1],
            ALU.is_equal, ALU.mult),
            deps=RNK.rdeps() + MSK.rdeps() + C.iotat.rdeps() + (PALL.wdeps() if tc == 0 else []))
        RNK.read(tp)
        MSK.read(tp)
        C.iotat.read(tp)
    PALL.wrote(p.barrier_tok("vector"))
    PG = C.pb[2]
    for dc in range(NDC):
        mm = None
        for tc in range(NTC):
            lhs = C.xnt.ap[:, 1024 * tc + 128 * dc: 1024 * tc + 128 * (dc + 1)]
            rhs = PALL.ap[:, 256 * tc:256 * (tc + 1)]
            mm = p.step("tensor", lambda e, lhs=lhs, rhs=rhs, tc=tc: e.matmul(
                PG.f32[:, 0:256], lhs, rhs, start=(tc == 0), stop=(tc == NTC - 1)),
                deps=(C.xnt.rdeps() + PALL.rdeps() + PG.buf.wdeps()) if tc == 0 else ())
            C.xnt.read(mm)
            PALL.read(mm)
        PG.buf.wrote(mm)
        te = p.step("scalar", lambda e, dc=dc: e.copy(SELF.ap[:, 256 * dc:256 * (dc + 1)], PG.f32[:, 0:256]),
                    deps=[mm] + (SELF.wdeps() if dc == 0 else []))
        PG.buf.read(te)
    SELF.wrote(p.barrier_tok("scalar"))


def emit_l1_attn(C, b):
    p = C.p
    dram = C.dram
    ep = C.epoch
    SELF = C.l1_self

    o = 0
    COSR = Buf(C.carve(o, 2048), "COSR", ep); o += 2048
    SINR = Buf(C.carve(o, 2048), "SINR", ep); o += 2048
    MST = [None] * 4
    for j in range(4):
        MST[j] = Buf(C.carve(o, 512), f"MST{j}", ep); o += 512
    NQF = Buf(C.carve(o, 2048), "NQF", ep); o += 2048
    CKVF = Buf(C.carve(o, 2048), "CKVF", ep); o += 2048
    KR1R = Buf(C.carve(o, 2048), "KR1R", ep); o += 2048
    KR2R = Buf(C.carve(o, 2048), "KR2R", ep); o += 2048
    WCQ = Buf(C.carve(o, 768), "WCQ", ep); o += 768
    WCKV = Buf(C.carve(o, 256), "WCKV", ep); o += 256
    WKR = Buf(C.carve(o, 512), "WKR", ep); o += 512
    WDQN = Buf(C.carve(o, 512), "WDQN", ep); o += 512
    WDQR = Buf(C.carve(o, 1024), "WDQR", ep); o += 1024
    WDKN = Buf(C.carve(o, 512), "WDKN", ep); o += 512
    WDV = Buf(C.carve(o, 512), "WDV", ep); o += 512
    WGT = Buf(C.carve(o, 24), "WGT", ep); o += 24
    GR96 = Buf(C.carve(o, 2 * 96), "GR96", ep); o += 2 * 96
    GR32 = Buf(C.carve(o, 2 * 32), "GR32", ep); o += 2 * 32
    GSC = Buf(C.carve(o, 2 * 128), "GSC", ep); o += 2 * 128
    GBC = Buf(C.carve(o, 2 * 4), "GBC", ep); o += 2 * 4
    TMPA = Buf(C.carve(o, 2048), "TMPA", ep); o += 2048
    TMPB = Buf(C.carve(o, 2048), "TMPB", ep); o += 2048
    KNG = Buf(C.carve(o, 2048), "KNG", ep); o += 2048
    V1T = Buf(C.carve(o, 16 * 132), "V1T", ep); o += 16 * 132
    KWN = Buf(C.carve(o, 2048), "KWN", ep); o += 2048
    KWR1 = Buf(C.carve(o, 2048), "KWR1", ep); o += 2048
    KWR2 = Buf(C.carve(o, 2048), "KWR2", ep); o += 2048
    VWT = Buf(C.carve(o, 16 * 132), "VWT", ep); o += 16 * 132
    KSN = Buf(C.carve(o, 256), "KSN", ep); o += 256
    KSR1 = Buf(C.carve(o, 256), "KSR1", ep); o += 256
    KSR2 = Buf(C.carve(o, 256), "KSR2", ep); o += 256
    VST = Buf(C.carve(o, 2 * 132), "VST", ep); o += 2 * 132
    WSLAB = Buf(C.carve(o, 4096), "WSLAB", ep); o += 4096
    QNT = Buf(C.carve(o, 512), "QNT", ep); o += 512
    QR1T = Buf(C.carve(o, 512), "QR1T", ep); o += 512
    QR2T = Buf(C.carve(o, 512), "QR2T", ep); o += 512
    OA = Buf(C.carve(o, 2048), "OA", ep); o += 2048
    RU = OA
    PSB = [None] * 4
    for hl in range(4):
        PSB[hl] = Buf(C.carve(o, 512), f"PSB{hl}", ep); o += 512
    ATTG = Buf(C.carve(o, 2048), "ATTG", ep); o += 2048
    WPG = Buf(C.carve(o, 1024), "WPG", ep); o += 1024
    AFM = Buf(C.carve(o, 128), "AFM", ep); o += 128
    RC = Buf(C.carve(o, 2 * 8), "RC", ep); o += 2 * 8
    assert o <= C.RSV0, o
    g96 = GR96.ap.bitcast(F32)
    g32 = GR32.ap.bitcast(F32)
    gsc = GSC.ap.bitcast(F32)
    gbc = GBC.ap.bitcast(F32)
    rcf = RC.ap.bitcast(F32)

    wsem = p.new_sem(f"l1w_{b}")

    fns = []

    def ld(dst_ap, src_ap):
        fns.append(lambda e, d=dst_ap, s=src_ap: e.dma_start(out=d, in_=s))

    for dc in range(NDC):
        ld(WCQ.ap[:, 96 * dc:96 * (dc + 1)], dram["L1_wcq"][128 * dc:128 * (dc + 1), :])
        ld(WCKV.ap[:, 32 * dc:32 * (dc + 1)], dram["L1_wckv"][128 * dc:128 * (dc + 1), :])
        ld(WKR.ap[:, 64 * dc:64 * (dc + 1)], dram["L1_wkr"][128 * dc:128 * (dc + 1), :])
        ld(WGT.ap[:, 3 * dc:3 * (dc + 1)], dram["L1_wgate"][128 * dc:128 * (dc + 1), :])
    ld(WDQN.ap[0:96, :], dram["L1_wdqn"])
    ld(WDQR.ap[0:96, :], dram["L1_wdqr"])
    ld(WDKN.ap[0:32, :], dram["L1_wdkn"])
    ld(WDV.ap[0:32, :], dram["L1_wdv"])
    wtok = p.dma_group("gpsimd", fns, wsem,
                       deps=WCQ.wdeps() + WCKV.wdeps() + WKR.wdeps() + WGT.wdeps() +
                            WDQN.wdeps() + WDQR.wdeps() + WDKN.wdeps() + WDV.wdeps())
    for bb in (WCQ, WCKV, WKR, WGT, WDQN, WDQR, WDKN, WDV):
        bb.wrote(wtok)
    gsem = p.new_sem(f"l1g_{b}")
    t_g1 = p.step("sync", lambda e: e.dma_start(out=gsc[0:1, 0:96], in_=dram["L1_gq"]),
                  deps=GSC.wdeps(), dma=True, sem=gsem)
    t_g2 = p.step("sync", lambda e: e.dma_start(out=gsc[0:1, 96:128], in_=dram["L1_gkv"]),
                  deps=[t_g1], dma=True, sem=gsem)
    GSC.wrote(t_g2)
    PB0, PB1, PB2, PB3 = C.pb[0], C.pb[1], C.pb[2], C.pb[3]
    tb1 = p.step("tensor", lambda e: e.matmul(PB0.f32[:, 0:96], C.ones_row_f.ap, gsc[0:1, 0:96],
                                              start=True, stop=True),
                 deps=GSC.rdeps() + PB0.buf.wdeps() + C.ones_row_f.rdeps())
    PB0.buf.wrote(tb1)
    GSC.read(tb1)
    tb2 = p.step("vector", lambda e: e.tensor_copy(g96, PB0.f32[:, 0:96]), deps=[tb1] + GR96.wdeps())
    PB0.buf.read(tb2)
    GR96.wrote(tb2)
    tb3 = p.step("tensor", lambda e: e.matmul(PB1.f32[:, 0:32], C.ones_row_f.ap, gsc[0:1, 96:128],
                                              start=True, stop=True),
                 deps=GSC.rdeps() + PB1.buf.wdeps())
    PB1.buf.wrote(tb3)
    GSC.read(tb3)
    tb4 = p.step("vector", lambda e: e.tensor_copy(g32, PB1.f32[:, 0:32]), deps=[tb3] + GR32.wdeps())
    PB1.buf.read(tb4)
    GR32.wrote(tb4)

    oaf = OA.ap.bitcast(F32)
    ti1 = p.step("gpsimd", lambda e: e.iota(oaf[0:1, 672:704], [[1, 32]], base=0, channel_multiplier=0,
                                            allow_small_or_imprecise_dtypes=True), deps=OA.wdeps())
    ti2 = p.step("scalar", lambda e: e.activation(oaf[0:1, 640:672], oaf[0:1, 672:704], AF.Exp,
                                                  scale=-2.0 * math.log(10000.0) / 64.0), deps=[ti1])
    cps = []
    for j in range(4):
        cps.append(p.step("vector", lambda e, j=j: e.tensor_copy(oaf[0:1, 32 * j:32 * (j + 1)],
                                                                 oaf[0:1, 640:672]), deps=[ti2]))
    OA.wrote(cps[-1])
    for tt in range(4):
        ti3 = p.step("gpsimd", lambda e, tt=tt: e.iota(oaf[0:1, 128:640], [[1, 512]], base=512 * tt,
                                                       channel_multiplier=0, allow_small_or_imprecise_dtypes=True),
                     deps=OA.wdeps() + [cps[-1]])
        OA.wrote(ti3)
        tm1 = p.step("tensor", lambda e: e.matmul(PB2.f32, oaf[0:1, 0:128], oaf[0:1, 128:640],
                                                  start=True, stop=True),
                     deps=[ti3] + PB2.buf.wdeps())
        PB2.buf.wrote(tm1)
        OA.read(tm1)
        tmaf = TMPA.ap.bitcast(F32)
        tmbf = TMPB.ap.bitcast(F32)
        r1 = p.step("vector", lambda e: e.tensor_scalar_mul(tmaf[:, 0:512], PB2.f32, 1.0 / (2 * math.pi)),
                    deps=[tm1] + TMPA.wdeps())
        r2 = p.step("vector", lambda e: e.tensor_copy(tmaf[:, 512:1024].bitcast(I32)[:, 0:512], tmaf[:, 0:512]),
                    deps=[r1])
        r3 = p.step("vector", lambda e: e.tensor_copy(tmaf[:, 0:512], tmaf[:, 512:1024].bitcast(I32)[:, 0:512]),
                    deps=[r2])
        r4 = p.step("vector", lambda e: e.scalar_tensor_tensor(tmbf[:, 0:512], tmaf[:, 0:512],
                                                               -2.0 * math.pi, PB2.f32, ALU.mult, ALU.add),
                    deps=[r3] + TMPB.wdeps())
        PB2.buf.read(r4)
        r5 = p.step("vector", lambda e: e.tensor_scalar(tmbf[:, 0:512], tmbf[:, 0:512],
                                                        math.pi, -math.pi, ALU.min, ALU.max), deps=[r4])
        TMPB.wrote(r5)
        TMPA.wrote(r3)
        s1 = p.step("scalar", lambda e, tt=tt: e.activation(SINR.ap[:, 512 * tt:512 * (tt + 1)],
                                                            tmbf[:, 0:512], AF.Sin),
                    deps=[r5] + (SINR.wdeps() if tt == 0 else []))
        TMPB.read(s1)
        c1 = p.step("vector", lambda e: e.tensor_scalar_add(tmaf[:, 0:512], tmbf[:, 0:512], math.pi / 2.0),
                    deps=[r5] + TMPA.wdeps())
        c2 = p.step("vector", lambda e: e.tensor_scalar(tmaf[:, 512:1024], tmaf[:, 0:512], math.pi,
                                                        None, ALU.is_gt), deps=[c1])
        c3 = p.step("vector", lambda e: e.scalar_tensor_tensor(tmaf[:, 0:512], tmaf[:, 512:1024],
                                                               -2.0 * math.pi, tmaf[:, 0:512],
                                                               ALU.mult, ALU.add), deps=[c2])
        TMPA.wrote(c3)
        s2 = p.step("scalar", lambda e, tt=tt: e.activation(COSR.ap[:, 512 * tt:512 * (tt + 1)],
                                                            tmaf[:, 0:512], AF.Sin),
                    deps=[c3] + (COSR.wdeps() if tt == 0 else []))
        TMPA.read(s2)
    SINR.wrote(p.barrier_tok("scalar"))
    COSR.wrote(p.barrier_tok("scalar"))
    ms0 = p.step("gpsimd", lambda e: e.iota(TMPA.ap.bitcast(F32)[:, 0:512], [[1, 512]], base=0,
                                            channel_multiplier=0, allow_small_or_imprecise_dtypes=True),
                 deps=TMPA.wdeps())
    TMPA.wrote(ms0)
    ms1 = p.step("gpsimd", lambda e: e.iota(rcf[:, 0:1], [[0, 1]], base=0, channel_multiplier=1,
                                            allow_small_or_imprecise_dtypes=True), deps=RC.wdeps())
    RC.wrote(ms1)
    for j in range(4):
        msj = p.step("vector", lambda e, j=j: e.tensor_scalar(rcf[:, 1:2], rcf[:, 0:1], float(128 * j),
                                                              None, ALU.add), deps=[ms1] + RC.wdeps())
        RC.wrote(msj)
        msk = p.step("vector", lambda e, j=j: e.tensor_scalar(MST[j].ap, TMPA.ap.bitcast(F32)[:, 0:512],
                                                              rcf[:, 1:2], None, ALU.is_ge),
                     deps=[msj, ms0] + MST[j].wdeps())
        MST[j].wrote(msk)
        TMPA.read(msk)
        RC.read(msk)

    def rms_lora(WB, wcols, n, grep, eps, DSTF, name):
        for tc in range(NTC):
            mm = None
            deps0 = C.xnf.rdeps() + WB.rdeps() + PB0.buf.wdeps()
            for dc in range(NDC):
                lhs = C.xnf.ap[:, 2048 * dc + 128 * tc: 2048 * dc + 128 * (tc + 1)]
                rhs = WB.ap[:, wcols * dc: wcols * dc + n]
                mm = p.step("tensor", lambda e, lhs=lhs, rhs=rhs, dc=dc: e.matmul(
                    PB0.f32[:, 0:n], lhs, rhs, start=(dc == 0), stop=(dc == NDC - 1)),
                    deps=deps0 if dc == 0 else ())
                WB.read(mm)
                C.xnf.read(mm)
            PB0.buf.wrote(mm)
            q1 = p.step("scalar", lambda e: e.activation(gsc[:, 8:8 + n], PB0.f32[:, 0:n], AF.Square,
                                                         accum_out=rcf[:, 2:3]),
                        deps=[mm] + GSC.wdeps() + RC.wdeps())
            GSC.wrote(q1)
            RC.wrote(q1)
            q2 = p.step("vector", lambda e: e.tensor_scalar(rcf[:, 3:4], rcf[:, 2:3], 1.0 / n, eps,
                                                            ALU.mult, ALU.add), deps=[q1])
            q3 = p.step("scalar", lambda e: e.activation(rcf[:, 3:4], rcf[:, 3:4], AF.Sqrt), deps=[q2])
            q4 = p.step("vector", lambda e: e.reciprocal(rcf[:, 3:4], rcf[:, 3:4]), deps=[q3])
            RC.wrote(q4)
            q5 = p.step("vector", lambda e: e.tensor_scalar(gsc[:, 8:8 + n], PB0.f32[:, 0:n], rcf[:, 3:4],
                                                            None, ALU.mult), deps=[q4, mm] + GSC.wdeps())
            PB0.buf.read(q5)
            q6 = p.step("vector", lambda e: e.tensor_tensor(gsc[:, 8:8 + n], gsc[:, 8:8 + n], grep,
                                                            ALU.mult), deps=[q5])
            GSC.wrote(q6)
            tt_ = p.step("tensor", lambda e: e.transpose(PB1.f32[0:n, 0:128], gsc[:, 8:8 + n],
                                                         C.ident_f32.ap),
                         deps=[q6] + PB1.buf.wdeps())
            PB1.buf.wrote(tt_)
            GSC.read(tt_)
            te = p.step("scalar", lambda e, tc=tc: e.copy(DSTF.ap[0:n, 128 * tc:128 * (tc + 1)],
                                                          PB1.f32[0:n, 0:128]),
                        deps=[tt_] + (DSTF.wdeps() if tc == 0 else []))
            PB1.buf.read(te)
        DSTF.wrote(p.barrier_tok("scalar"))

    rms_lora(WCQ, 96, 96, g96[:, 0:96], 1e-6, NQF, "nq")
    rms_lora(WCKV, 32, 32, g32[:, 0:32], 1e-6, CKVF, "ckv")

    for tt in range(4):
        mm = None
        deps0 = C.xnf.rdeps() + WKR.rdeps() + PB0.buf.wdeps()
        for dc in range(NDC):
            lhs = WKR.ap[:, 64 * dc:64 * (dc + 1)]
            rhs = C.xnf.ap[:, 2048 * dc + 512 * tt: 2048 * dc + 512 * (tt + 1)]
            mm = p.step("tensor", lambda e, lhs=lhs, rhs=rhs, dc=dc: e.matmul(
                PB0.f32[0:64, :], lhs, rhs, start=(dc == 0), stop=(dc == NDC - 1)),
                deps=deps0 if dc == 0 else ())
            WKR.read(mm)
            C.xnf.read(mm)
        PB0.buf.wrote(mm)
        e1 = p.step("scalar", lambda e, tt=tt: e.activation(TMPA.ap[0:32, 512 * tt:512 * (tt + 1)],
                                                            PB0.f32[0:32, :], AF.Copy, scale=1.0 / 16.0),
                    deps=[mm] + (TMPA.wdeps() if tt == 0 else []))
        e2 = p.step("scalar", lambda e, tt=tt: e.activation(TMPB.ap[0:32, 512 * tt:512 * (tt + 1)],
                                                            PB0.f32[32:64, :], AF.Copy, scale=1.0 / 16.0),
                    deps=[mm] + (TMPB.wdeps() if tt == 0 else []))
        PB0.buf.read(e1)
        PB0.buf.read(e2)
    TMPA.wrote(p.barrier_tok("scalar"))
    TMPB.wrote(p.barrier_tok("scalar"))

    def rope_apply(x1, x2, o1, o2, coss, sins, n0, ncols, rows, scratch):
        cs = coss.ap[0:rows, n0:n0 + ncols]
        sn = sins.ap[0:rows, n0:n0 + ncols]
        u = scratch.ap[0:rows, 0:ncols]
        t1 = p.step("vector", lambda e: e.tensor_tensor(o1.sl, x1.sl, cs, ALU.mult),
                    deps=x1.deps + coss.rdeps() + o1.wdeps)
        t2 = p.step("vector", lambda e: e.tensor_tensor(u, x2.sl, sn, ALU.mult),
                    deps=x2.deps + sins.rdeps() + scratch.wdeps())
        t3 = p.step("vector", lambda e: e.tensor_tensor(o1.sl, o1.sl, u, ALU.subtract), deps=[t1, t2])
        t4 = p.step("vector", lambda e: e.tensor_tensor(o2.sl, x2.sl, cs, ALU.mult),
                    deps=x2.deps + coss.rdeps() + o2.wdeps)
        t5 = p.step("vector", lambda e: e.tensor_tensor(u, x1.sl, sn, ALU.mult),
                    deps=x1.deps + sins.rdeps() + [t3])
        scratch.wrote(t5)
        t6 = p.step("vector", lambda e: e.tensor_tensor(o2.sl, o2.sl, u, ALU.add), deps=[t4, t5])
        scratch.read(t6)
        coss.read(t6)
        sins.read(t6)
        return t3, t6

    class _V:
        def __init__(self, sl, deps, wdeps=()):
            self.sl = sl
            self.deps = list(deps) if deps else []
            self.wdeps = list(wdeps) if wdeps else []

    ta, tb6 = rope_apply(_V(TMPA.ap[0:32, :], TMPA.rdeps()), _V(TMPB.ap[0:32, :], TMPB.rdeps()),
                         _V(KR1R.ap[0:32, :], None, KR1R.wdeps()), _V(KR2R.ap[0:32, :], None, KR2R.wdeps()),
                         COSR, SINR, 0, 2048, 32, RU)
    KR1R.wrote(ta)
    KR2R.wrote(tb6)
    for (KB, dst) in ((KR1R, KR1R), (KR2R, KR2R)):
        for tt in range(4):
            mm = p.step("tensor", lambda e, KB=KB, tt=tt: e.matmul(
                PB0.f32[:, 0:512], C.rep4.ap, KB.ap[0:32, 512 * tt:512 * (tt + 1)],
                start=True, stop=True),
                deps=KB.rdeps() + PB0.buf.wdeps() + C.rep4.rdeps())
            PB0.buf.wrote(mm)
            KB.read(mm)
            te = p.step("vector", lambda e, dst=dst, tt=tt: e.tensor_copy(
                dst.ap[:, 512 * tt:512 * (tt + 1)], PB0.f32[:, 0:512]),
                deps=[mm] + dst.wdeps())
            PB0.buf.read(te)
            dst.wrote(te)

    gz = p.step("vector", lambda e: e.memset(gsc[:, 28:31], 0.0), deps=GSC.wdeps())
    GSC.wrote(gz)
    for tc in range(NTC):
        mm = None
        deps0 = C.xnf.rdeps() + WGT.rdeps() + PB2.buf.wdeps()
        for dc in range(NDC):
            lhs = C.xnf.ap[:, 2048 * dc + 128 * tc: 2048 * dc + 128 * (tc + 1)]
            rhs = WGT.ap[:, 3 * dc:3 * (dc + 1)]
            mm = p.step("tensor", lambda e, lhs=lhs, rhs=rhs, dc=dc: e.matmul(
                PB2.f32[:, 0:3], lhs, rhs, start=(dc == 0), stop=(dc == NDC - 1)),
                deps=deps0 if dc == 0 else ())
            WGT.read(mm)
            C.xnf.read(mm)
        PB2.buf.wrote(mm)
        ga = p.step("vector", lambda e: e.tensor_tensor(gsc[:, 28:31], gsc[:, 28:31], PB2.f32[:, 0:3],
                                                        ALU.add), deps=[mm] + GSC.rdeps() + GSC.wdeps())
        GSC.wrote(ga)
        PB2.buf.read(ga)
    gm = p.step("tensor", lambda e: e.matmul(PB2.f32[0:1, 4:7], C.ones_col_f.ap, gsc[:, 28:31],
                                             start=True, stop=True),
                deps=GSC.rdeps() + PB2.buf.wdeps() + C.ones_col_f.rdeps())
    PB2.buf.wrote(gm)
    GSC.read(gm)
    g1 = p.step("vector", lambda e: e.tensor_scalar_mul(gsc[0:1, 31:34], PB2.f32[0:1, 4:7], 1.0 / 2048.0),
                deps=[gm] + GSC.wdeps())
    PB2.buf.read(g1)
    g2 = p.step("vector", lambda e: e.reduce_max(gsc[0:1, 34:35], gsc[0:1, 31:34], AX.X), deps=[g1])
    g3 = p.step("vector", lambda e: e.tensor_scalar_mul(gsc[0:1, 35:36], gsc[0:1, 34:35], -1.0), deps=[g2])
    g4 = p.step("scalar", lambda e: e.activation(gsc[0:1, 31:34], gsc[0:1, 31:34], AF.Exp,
                                                 bias=gsc[0:1, 35:36], accum_out=gsc[0:1, 36:37]), deps=[g3])
    g5 = p.step("vector", lambda e: e.reciprocal(gsc[0:1, 37:38], gsc[0:1, 36:37]), deps=[g4])
    g6 = p.step("vector", lambda e: e.tensor_scalar(gsc[0:1, 31:34], gsc[0:1, 31:34], gsc[0:1, 37:38],
                                                    None, ALU.mult), deps=[g5])
    GSC.wrote(g6)
    for br in range(3):
        gb1 = p.step("tensor", lambda e, br=br: e.matmul(PB2.f32[:, 8 + br:9 + br], C.ones_row_f.ap,
                                                         gsc[0:1, 31 + br:32 + br], start=True, stop=True),
                     deps=[g6] + PB2.buf.wdeps())
        PB2.buf.wrote(gb1)
        GSC.read(gb1)
        gb2 = p.step("vector", lambda e, br=br: e.tensor_copy(gbc[:, br:br + 1], PB2.f32[:, 8 + br:9 + br]),
                     deps=[gb1] + (GBC.wdeps() if br == 0 else []))
        PB2.buf.read(gb2)
    GBC.wrote(p.barrier_tok("vector"))

    PAB = [C.pb[0], C.pb[1], C.pb[2], C.pb[3]]
    PO = [C.pb[4], C.pb[5], C.pb[6], C.pb[7]]
    wg_sem = p.new_sem(f"l1wg_{b}")

    def modeA_to(dstbuf, dstsl_fn, Wb, wcolbase, wperdc, n_out_rows, ntiles, deps_extra=(), scale=None):
        for tt in range(ntiles):
            mm = None
            deps0 = C.xnf.rdeps() + Wb.rdeps() + PB0.buf.wdeps() + list(deps_extra)
            for dc in range(NDC):
                lhs = Wb.ap[:, wperdc * dc + wcolbase: wperdc * dc + wcolbase + n_out_rows]
                rhs = C.xnf.ap[:, 2048 * dc + 512 * tt: 2048 * dc + 512 * (tt + 1)]
                mm = p.step("tensor", lambda e, lhs=lhs, rhs=rhs, dc=dc, n=n_out_rows: e.matmul(
                    PB0.f32[0:n, :], lhs, rhs, start=(dc == 0), stop=(dc == NDC - 1)),
                    deps=deps0 if dc == 0 else ())
                Wb.read(mm)
                C.xnf.read(mm)
            PB0.buf.wrote(mm)
            dst = dstsl_fn(tt)
            if scale is None:
                te = p.step("scalar", lambda e, dst=dst, n=n_out_rows: e.copy(dst, PB0.f32[0:n, :]),
                            deps=[mm] + dstbuf.wdeps())
            else:
                te = p.step("scalar", lambda e, dst=dst, n=n_out_rows, s=scale: e.activation(
                    dst, PB0.f32[0:n, :], AF.Copy, scale=s), deps=[mm] + dstbuf.wdeps())
            PB0.buf.read(te)
            dstbuf.wrote(te)

    for g in range(4):
        for tt in range(4):
            mm = p.step("tensor", lambda e, g=g, tt=tt: e.matmul(
                PB0.f32, WDKN.ap[0:32, 128 * g:128 * (g + 1)],
                CKVF.ap[0:32, 512 * tt:512 * (tt + 1)], start=True, stop=True),
                deps=WDKN.rdeps() + CKVF.rdeps() + PB0.buf.wdeps())
            PB0.buf.wrote(mm)
            WDKN.read(mm)
            CKVF.read(mm)
            te = p.step("scalar", lambda e, tt=tt: e.copy(KNG.ap[:, 512 * tt:512 * (tt + 1)], PB0.f32),
                        deps=[mm] + KNG.wdeps())
            PB0.buf.read(te)
            KNG.wrote(te)
        for tc in range(NTC):
            mm = p.step("tensor", lambda e, g=g, tc=tc: e.matmul(
                PB1.f32[:, 0:128], CKVF.ap[0:32, 128 * tc:128 * (tc + 1)],
                WDV.ap[0:32, 128 * g:128 * (g + 1)], start=True, stop=True),
                deps=CKVF.rdeps() + WDV.rdeps() + PB1.buf.wdeps())
            PB1.buf.wrote(mm)
            CKVF.read(mm)
            WDV.read(mm)
            dst = V1T.ap[:, 132 * tc:132 * (tc + 1)].rearrange("p (h v) -> p h v", h=4)[:, :, 0:32]
            src = PB1.f32[:, 0:128].rearrange("p (h v) -> p h v", h=4)
            te = p.step("vector", lambda e, dst=dst, src=src: e.tensor_copy(dst, src),
                        deps=[mm] + V1T.wdeps())
            PB1.buf.read(te)
            to = p.step("vector", lambda e, tc=tc: e.memset(
                V1T.ap[:, 132 * tc:132 * (tc + 1)].rearrange("p (h v) -> p h v", h=4)[:, :, 32:33], 1.0),
                deps=[te])
            V1T.wrote(to)
        grp_bufs = {}

        def stream_group_w(names_cols, deps=()):
            fns2 = []
            off = 0
            res = {}
            for (nm, ncol) in names_cols:
                for dc in range(NDC):
                    dst = WSLAB.ap[:, off + ncol * dc: off + ncol * dc + ncol]
                    srcw = dram[nm][128 * dc:128 * (dc + 1), 128 * g if ncol == 128 else 256 * g:
                                    (128 * g if ncol == 128 else 256 * g) + ncol]
                    fns2.append(lambda e, d=dst, s=srcw: e.dma_start(out=d, in_=s))
                res[nm] = off
                off += ncol * NDC
            tok = p.dma_group("gpsimd", fns2, wg_sem, deps=WSLAB.wdeps() + list(deps))
            WSLAB.wrote(tok)
            return res

        for br in L1_BRANCHES:
            if br == 0:
                KB_n, KB_r1, KB_r2, VB = KNG, KR1R, KR2R, V1T
                nkt = 16
                causal = True
            elif br == 1:
                offs = stream_group_w([("L1_wselk_n", 128), ("L1_wselk_r", 256), ("L1_wselv", 128)])
                mm = None
                for dc in range(NDC):
                    lhs = WSLAB.ap[:, offs["L1_wselk_n"] + 128 * dc: offs["L1_wselk_n"] + 128 * (dc + 1)]
                    rhs = SELF.ap[:, 256 * dc:256 * (dc + 1)]
                    mm = p.step("tensor", lambda e, lhs=lhs, rhs=rhs, dc=dc: e.matmul(
                        PB0.f32[:, 0:256], lhs, rhs, start=(dc == 0), stop=(dc == NDC - 1)),
                        deps=(WSLAB.rdeps() + SELF.rdeps() + PB0.buf.wdeps()) if dc == 0 else ())
                    WSLAB.read(mm)
                    SELF.read(mm)
                PB0.buf.wrote(mm)
                te = p.step("scalar", lambda e: e.copy(KSN.ap[:, 0:256], PB0.f32[:, 0:256]),
                            deps=[mm] + KSN.wdeps())
                PB0.buf.read(te)
                KSN.wrote(te)
                for half, dstb in ((0, TMPA), (1, TMPB)):
                    mm = None
                    for dc in range(NDC):
                        lhs = WSLAB.ap[:, offs["L1_wselk_r"] + 256 * dc + 128 * half:
                                       offs["L1_wselk_r"] + 256 * dc + 128 * (half + 1)]
                        rhs = SELF.ap[:, 256 * dc:256 * (dc + 1)]
                        mm = p.step("tensor", lambda e, lhs=lhs, rhs=rhs, dc=dc: e.matmul(
                            PB0.f32[:, 0:256], lhs, rhs, start=(dc == 0), stop=(dc == NDC - 1)),
                            deps=(WSLAB.rdeps() + SELF.rdeps() + PB0.buf.wdeps()) if dc == 0 else ())
                        WSLAB.read(mm)
                        SELF.read(mm)
                    PB0.buf.wrote(mm)
                    te = p.step("scalar", lambda e, dstb=dstb: e.copy(dstb.ap[:, 0:256], PB0.f32[:, 0:256]),
                                deps=[mm] + dstb.wdeps())
                    PB0.buf.read(te)
                    dstb.wrote(te)
                ta_, tb_ = rope_apply(_V(TMPA.ap[:, 0:256], TMPA.rdeps()), _V(TMPB.ap[:, 0:256], TMPB.rdeps()),
                                      _V(KSR1.ap[:, 0:256], None, KSR1.wdeps()),
                                      _V(KSR2.ap[:, 0:256], None, KSR2.wdeps()),
                                      COSR, SINR, 0, 256, 128, RU)
                KSR1.wrote(ta_)
                KSR2.wrote(tb_)
                for s in range(2):
                    mm = None
                    for dc in range(NDC):
                        lhs = SELF.ap[:, 256 * dc + 128 * s: 256 * dc + 128 * (s + 1)]
                        rhs = WSLAB.ap[:, offs["L1_wselv"] + 128 * dc: offs["L1_wselv"] + 128 * (dc + 1)]
                        mm = p.step("tensor", lambda e, lhs=lhs, rhs=rhs, dc=dc: e.matmul(
                            PB1.f32[:, 0:128], lhs, rhs, start=(dc == 0), stop=(dc == NDC - 1)),
                            deps=(WSLAB.rdeps() + SELF.rdeps() + PB1.buf.wdeps()) if dc == 0 else ())
                        WSLAB.read(mm)
                        SELF.read(mm)
                    PB1.buf.wrote(mm)
                    dst = VST.ap[:, 132 * s:132 * (s + 1)].rearrange("p (h v) -> p h v", h=4)[:, :, 0:32]
                    srcv = PB1.f32[:, 0:128].rearrange("p (h v) -> p h v", h=4)
                    te = p.step("vector", lambda e, dst=dst, srcv=srcv: e.tensor_copy(dst, srcv),
                                deps=[mm] + VST.wdeps())
                    PB1.buf.read(te)
                    to = p.step("vector", lambda e, s=s: e.memset(
                        VST.ap[:, 132 * s:132 * (s + 1)].rearrange("p (h v) -> p h v", h=4)[:, :, 32:33], 1.0),
                        deps=[te])
                    VST.wrote(to)
                KB_n, KB_r1, KB_r2, VB = KSN, KSR1, KSR2, VST
                nkt = 2
                causal = False
            else:
                offs = stream_group_w([("L1_wink_n", 128), ("L1_wink_r", 256), ("L1_winv", 128)])
                for tt in range(4):
                    mm = None
                    for dc in range(NDC):
                        lhs = WSLAB.ap[:, offs["L1_wink_n"] + 128 * dc: offs["L1_wink_n"] + 128 * (dc + 1)]
                        rhs = C.xnf.ap[:, 2048 * dc + 512 * tt: 2048 * dc + 512 * (tt + 1)]
                        mm = p.step("tensor", lambda e, lhs=lhs, rhs=rhs, dc=dc: e.matmul(
                            PB0.f32, lhs, rhs, start=(dc == 0), stop=(dc == NDC - 1)),
                            deps=(WSLAB.rdeps() + C.xnf.rdeps() + PB0.buf.wdeps()) if dc == 0 else ())
                        WSLAB.read(mm)
                        C.xnf.read(mm)
                    PB0.buf.wrote(mm)
                    te = p.step("scalar", lambda e, tt=tt: e.copy(KWN.ap[:, 512 * tt:512 * (tt + 1)], PB0.f32),
                                deps=[mm] + KWN.wdeps())
                    PB0.buf.read(te)
                    KWN.wrote(te)
                for half, dstb in ((0, TMPA), (1, TMPB)):
                    for tt in range(4):
                        mm = None
                        for dc in range(NDC):
                            lhs = WSLAB.ap[:, offs["L1_wink_r"] + 256 * dc + 128 * half:
                                           offs["L1_wink_r"] + 256 * dc + 128 * (half + 1)]
                            rhs = C.xnf.ap[:, 2048 * dc + 512 * tt: 2048 * dc + 512 * (tt + 1)]
                            mm = p.step("tensor", lambda e, lhs=lhs, rhs=rhs, dc=dc: e.matmul(
                                PB0.f32, lhs, rhs, start=(dc == 0), stop=(dc == NDC - 1)),
                                deps=(WSLAB.rdeps() + C.xnf.rdeps() + PB0.buf.wdeps()) if dc == 0 else ())
                            WSLAB.read(mm)
                            C.xnf.read(mm)
                        PB0.buf.wrote(mm)
                        te = p.step("scalar", lambda e, dstb=dstb, tt=tt: e.copy(
                            dstb.ap[:, 512 * tt:512 * (tt + 1)], PB0.f32), deps=[mm] + dstb.wdeps())
                        PB0.buf.read(te)
                        dstb.wrote(te)
                ta_, tb_ = rope_apply(_V(TMPA.ap[:, 0:2048], TMPA.rdeps()), _V(TMPB.ap[:, 0:2048], TMPB.rdeps()),
                                      _V(KWR1.ap[:, 0:2048], None, KWR1.wdeps()),
                                      _V(KWR2.ap[:, 0:2048], None, KWR2.wdeps()),
                                      COSR, SINR, 0, 2048, 128, RU)
                KWR1.wrote(ta_)
                KWR2.wrote(tb_)
                for tc in range(NTC):
                    mm = None
                    for dc in range(NDC):
                        lhs = C.xnf.ap[:, 2048 * dc + 128 * tc: 2048 * dc + 128 * (tc + 1)]
                        rhs = WSLAB.ap[:, offs["L1_winv"] + 128 * dc: offs["L1_winv"] + 128 * (dc + 1)]
                        mm = p.step("tensor", lambda e, lhs=lhs, rhs=rhs, dc=dc: e.matmul(
                            PB1.f32[:, 0:128], lhs, rhs, start=(dc == 0), stop=(dc == NDC - 1)),
                            deps=(WSLAB.rdeps() + C.xnf.rdeps() + PB1.buf.wdeps()) if dc == 0 else ())
                        WSLAB.read(mm)
                        C.xnf.read(mm)
                    PB1.buf.wrote(mm)
                    dst = VWT.ap[:, 132 * tc:132 * (tc + 1)].rearrange("p (h v) -> p h v", h=4)[:, :, 0:32]
                    srcv = PB1.f32[:, 0:128].rearrange("p (h v) -> p h v", h=4)
                    te = p.step("vector", lambda e, dst=dst, srcv=srcv: e.tensor_copy(dst, srcv),
                                deps=[mm] + VWT.wdeps())
                    PB1.buf.read(te)
                    to = p.step("vector", lambda e, tc=tc: e.memset(
                        VWT.ap[:, 132 * tc:132 * (tc + 1)].rearrange("p (h v) -> p h v", h=4)[:, :, 32:33], 1.0),
                        deps=[te])
                    VWT.wrote(to)
                KB_n, KB_r1, KB_r2, VB = KWN, KWR1, KWR2, VWT
                nkt = 16
                causal = True

            for tqi in range(4):
                mmq = p.step("tensor", lambda e, g=g, tqi=tqi: e.matmul(
                    PB0.f32, WDQN.ap[0:96, 128 * g:128 * (g + 1)],
                    NQF.ap[0:96, 512 * tqi:512 * (tqi + 1)], start=True, stop=True),
                    deps=WDQN.rdeps() + NQF.rdeps() + PB0.buf.wdeps())
                PB0.buf.wrote(mmq)
                WDQN.read(mmq)
                NQF.read(mmq)
                tq_ = p.step("scalar", lambda e: e.copy(QNT.ap[:, 0:512], PB0.f32),
                             deps=[mmq] + QNT.wdeps())
                PB0.buf.read(tq_)
                QNT.wrote(tq_)
                for half, dstb in ((0, TMPA), (1, TMPB)):
                    mm = p.step("tensor", lambda e, g=g, tqi=tqi, half=half: e.matmul(
                        PB0.f32, WDQR.ap[0:96, 256 * g + 128 * half: 256 * g + 128 * (half + 1)],
                        NQF.ap[0:96, 512 * tqi:512 * (tqi + 1)], start=True, stop=True),
                        deps=WDQR.rdeps() + NQF.rdeps() + PB0.buf.wdeps())
                    PB0.buf.wrote(mm)
                    WDQR.read(mm)
                    NQF.read(mm)
                    te = p.step("scalar", lambda e, dstb=dstb: e.copy(dstb.ap[:, 0:512], PB0.f32),
                                deps=[mm] + dstb.wdeps())
                    PB0.buf.read(te)
                    dstb.wrote(te)
                rope_apply(_V(TMPA.ap[:, 0:512], TMPA.rdeps()), _V(TMPB.ap[:, 0:512], TMPB.rdeps()),
                           _V(QR1T.ap[:, 0:512], None, QR1T.wdeps()),
                           _V(QR2T.ap[:, 0:512], None, QR2T.wdeps()),
                           COSR, SINR, 512 * tqi, 512, 128, RU)
                QR1T.wrote(p.barrier_tok("vector"))
                QR2T.wrote(p.barrier_tok("vector"))

                ktlist = list(range(min(nkt, 4 * tqi + 4))) if causal else list(range(nkt))
                po_started = [False] * 4
                for kti, kt in enumerate(ktlist):
                    last_kt = (kti == len(ktlist) - 1)
                    for hl in range(4):
                        rb = 32 * hl
                        pa = PAB[hl]
                        d0 = (KB_n.rdeps() + KB_r1.rdeps() + KB_r2.rdeps() + QNT.rdeps() +
                              QR1T.rdeps() + QR2T.rdeps() + pa.buf.wdeps())
                        mm1 = p.step("tensor", lambda e, pa=pa, rb=rb, kt=kt, kb=KB_n.ap: e.matmul(
                            pa.f32, kb[rb:rb + 32, 128 * kt:128 * (kt + 1)],
                            QNT.ap[rb:rb + 32, 0:512], start=True, stop=False,
                            tile_position=(rb, 0)), deps=d0)
                        mm2 = p.step("tensor", lambda e, pa=pa, rb=rb, kt=kt, kb=KB_r1.ap: e.matmul(
                            pa.f32, kb[rb:rb + 32, 128 * kt:128 * (kt + 1)],
                            QR1T.ap[rb:rb + 32, 0:512], start=False, stop=False,
                            tile_position=(rb, 0)))
                        mm3 = p.step("tensor", lambda e, pa=pa, rb=rb, kt=kt, kb=KB_r2.ap: e.matmul(
                            pa.f32, kb[rb:rb + 32, 128 * kt:128 * (kt + 1)],
                            QR2T.ap[rb:rb + 32, 0:512], start=False, stop=True,
                            tile_position=(rb, 0)))
                        pa.buf.wrote(mm3)
                        for bbb in (KB_n, KB_r1, KB_r2, QNT, QR1T, QR2T):
                            bbb.read(mm3)
                    for hl in range(4):
                        pa = PAB[hl]
                        tex = p.step("scalar", lambda e, hl=hl, pa=pa: e.activation(
                            PSB[hl].ap[:, 0:512], pa.f32, AF.Exp, scale=1.0 / math.sqrt(96.0),
                            bias=C.nshift.ap),
                            deps=pa.buf.rdeps() + PSB[hl].wdeps() + C.nshift.rdeps())
                        pa.buf.read(tex)
                        if causal and kt >= 4 * tqi:
                            j = kt - 4 * tqi
                            tmk = p.step("vector", lambda e, hl=hl, j=j: e.tensor_tensor(
                                PSB[hl].ap[:, 0:512], PSB[hl].ap[:, 0:512], MST[j].ap, ALU.mult),
                                deps=[tex] + MST[j].rdeps())
                            MST[j].read(tmk)
                            PSB[hl].wrote(tmk)
                        else:
                            PSB[hl].wrote(tex)
                    for hl in range(4):
                        po = PO[hl]
                        vsl = VB.ap[:, 132 * kt + 33 * hl: 132 * kt + 33 * (hl + 1)]
                        st = not po_started[hl]
                        po_started[hl] = True
                        mmv = p.step("tensor", lambda e, po=po, vsl=vsl, hl=hl, st=st, sp=last_kt: e.matmul(
                            po.f32[0:33, 0:512], vsl, PSB[hl].ap[:, 0:512], start=st, stop=sp),
                            deps=(po.buf.wdeps() if st else []) + VB.rdeps() + PSB[hl].rdeps())
                        po.buf.wrote(mmv)
                        VB.read(mmv)
                        PSB[hl].read(mmv)
                for hl in range(4):
                    po = PO[hl]
                    tev = p.step("scalar", lambda e, hl=hl, po=po: e.copy(
                        OA.ap[0:33, 512 * hl:512 * (hl + 1)], po.f32[0:33, 0:512]),
                        deps=po.buf.rdeps() + OA.wdeps())
                    po.buf.read(tev)
                    OA.wrote(tev)
                for s in range(4):
                    tcg = 4 * tqi + s
                    for hl in range(4):
                        pa = PAB[hl]
                        ttr = p.step("tensor", lambda e, pa=pa, hl=hl, s=s: e.transpose(
                            pa.bf16[:, 0:33], OA.ap[0:33, 512 * hl + 128 * s: 512 * hl + 128 * (s + 1)],
                            C.ident_bf.ap[0:33, 0:33]),
                            deps=OA.rdeps() + pa.buf.wdeps())
                        pa.buf.wrote(ttr)
                        OA.read(ttr)
                        tr1 = p.step("vector", lambda e, pa=pa, hl=hl: e.reciprocal(
                            rcf[:, 4 + hl:5 + hl], pa.bf16[:, 32:33]), deps=[ttr] + RC.wdeps())
                        RC.wrote(tr1)
                        tr2 = p.step("vector", lambda e, hl=hl, br=br: e.tensor_tensor(
                            rcf[:, 4 + hl:5 + hl], rcf[:, 4 + hl:5 + hl], gbc[:, br:br + 1], ALU.mult),
                            deps=[tr1] + GBC.rdeps())
                        GBC.read(tr2)
                        RC.wrote(tr2)
                        asl = ATTG.ap[:, 128 * tcg + 32 * hl: 128 * tcg + 32 * (hl + 1)]
                        if br == 0:
                            ta2 = p.step("vector", lambda e, pa=pa, asl=asl, hl=hl: e.tensor_scalar(
                                asl, pa.bf16[:, 0:32], rcf[:, 4 + hl:5 + hl], None, ALU.mult),
                                deps=[tr2] + ATTG.wdeps())
                        else:
                            ta2 = p.step("vector", lambda e, pa=pa, asl=asl, hl=hl: e.scalar_tensor_tensor(
                                asl, pa.bf16[:, 0:32], rcf[:, 4 + hl:5 + hl], asl, ALU.mult, ALU.add),
                                deps=[tr2] + ATTG.wdeps() + ATTG.rdeps())
                        ATTG.wrote(ta2)
                        RC.read(ta2)
                        pa.buf.read(ta2)
        fns3 = [lambda e, g=g: e.dma_start(out=WPG.ap[:, 0:1024],
                                           in_=dram["L1_wproj"][128 * g:128 * (g + 1), :])]
        wptok = p.dma_group("gpsimd", fns3, wg_sem, deps=WPG.wdeps())
        WPG.wrote(wptok)
        for tc in range(NTC):
            ttr = p.step("tensor", lambda e, tc=tc: e.transpose(
                PB0.bf16[:, 0:128], ATTG.ap[:, 128 * tc:128 * (tc + 1)], C.ident_bf.ap),
                deps=ATTG.rdeps() + PB0.buf.wdeps() + C.ident_bf.rdeps())
            PB0.buf.wrote(ttr)
            ATTG.read(ttr)
            te = p.step("vector", lambda e: e.tensor_copy(AFM.ap[:, 0:128], PB0.bf16[:, 0:128]),
                        deps=[ttr] + AFM.wdeps())
            PB0.buf.read(te)
            AFM.wrote(te)
            for nh in range(2):
                mm = p.step("tensor", lambda e, nh=nh: e.matmul(
                    PB1.f32, AFM.ap[:, 0:128], WPG.ap[:, 512 * nh:512 * (nh + 1)],
                    start=True, stop=True),
                    deps=AFM.rdeps() + WPG.rdeps() + PB1.buf.wdeps())
                PB1.buf.wrote(mm)
                AFM.read(mm)
                WPG.read(mm)
                xsl = C.xt.ap[:, 1024 * tc + 512 * nh: 1024 * tc + 512 * (nh + 1)]
                ta2 = p.step("vector", lambda e, xsl=xsl: e.tensor_tensor(xsl, xsl, PB1.f32, ALU.add),
                             deps=[mm] + C.xt.wdeps())
                C.xt.wrote(ta2)
                PB1.buf.read(ta2)


def emit_l1_rails(C, b):
    p = C.p
    dram = C.dram
    ep = C.epoch
    o = 0
    RMT = Buf(C.carve(o, 4096), "RMT", ep); o += 4096
    RUPH = Buf(C.carve(o, 4096), "RUPH", ep); o += 4096
    BDALL = Buf(C.carve(o, 32 * 512), "BDALL", ep); o += 32 * 512
    RMP = Buf(C.carve(o, 128), "RMP", ep); o += 128
    CPS = [None] * 3
    for i in range(3):
        CPS[i] = Buf(C.carve(o, 2048), f"CPS{i}", ep); o += 2048
    HT = Buf(C.carve(o, 512), "HT", ep); o += 512
    OE = Buf(C.carve(o, 512), "OE", ep); o += 512
    assert o <= C.RSV0, o

    wsem = p.new_sem(f"l1r_{b}")
    csem = [p.new_sem(f"l1c{i}_{b}") for i in range(3)]

    fns = [lambda e: e.dma_start(out=RMT.ap[0:16, :].rearrange("j (h r i) -> j h r i", h=4, r=64),
                                 in_=dram["L1_rmT"].transpose((2, 0, 1, 3))),
           ]
    wtok = p.dma_group("gpsimd", fns, wsem, deps=RMT.wdeps())
    RMT.wrote(wtok)

    PZ = C.pb[7]
    POC = [C.pb[i] for i in range(7)]

    for h in range(4):
        t_r = p.dma_group("gpsimd", [lambda e, h=h: e.dma_start(
            out=RUPH.ap[0:16, :].rearrange("j (r k) -> j r k", r=64),
            in_=dram["L1_rupT"][h].transpose((1, 0, 2)))],
            wsem, deps=RUPH.wdeps())
        RUPH.wrote(t_r)
        for c in range(8):
            bd_i = 8 * h + c
            for rr in range(8):
                r = 8 * c + rr
                tz = p.step("vector", lambda e: e.memset(RMP.ap[0:16, :], 0.0), deps=RMP.wdeps())
                tc_ = p.step("vector", lambda e, h=h, r=r, rr=rr: e.tensor_copy(
                    RMP.ap[0:16, 16 * rr:16 * (rr + 1)],
                    RMT.ap[0:16, 1024 * h + 16 * r:1024 * h + 16 * (r + 1)]),
                    deps=[tz] + RMT.rdeps())
                RMP.wrote(tc_)
                RMT.read(tc_)
                mm = p.step("tensor", lambda e, r=r, rr=rr: e.matmul(
                    PZ.f32[:, 64 * rr:64 * (rr + 1)], RMP.ap[0:16, :],
                    RUPH.ap[0:16, 64 * r:64 * (r + 1)], start=True, stop=True),
                    deps=RMP.rdeps() + RUPH.rdeps() + (PZ.buf.wdeps() if rr == 0 else []))
                PZ.buf.wrote(mm)
                RMP.read(mm)
                RUPH.read(mm)
            te = p.step("scalar", lambda e, bd_i=bd_i: e.copy(
                BDALL.ap[:, 512 * bd_i:512 * (bd_i + 1)], PZ.f32),
                deps=PZ.buf.rdeps() + [PZ.buf.w] + (BDALL.wdeps() if bd_i == 0 else []))
            PZ.buf.read(te)
    BDALL.wrote(p.barrier_tok("scalar"))

    for tm in range(4):
        for opass, oc_list in ((0, list(range(7))), (1, [7])):
            oc0 = oc_list[0]
            ncols = 128 * len(oc_list)
            for k in range(128):
                h = k // 32
                c = (k % 32) // 4
                zs = k % 4
                bd_i = 8 * h + c
                slot = CPS[k % 3]
                csrc = dram["L1_wcproj_t"][128 * k:128 * (k + 1), 128 * oc0:128 * oc0 + ncols]
                tl = p.dma_group("gpsimd", [lambda e, slot=slot, csrc=csrc, ncols=ncols: e.dma_start(
                    out=slot.ap[:, 0:ncols], in_=csrc)], csem[k % 3], deps=slot.wdeps())
                slot.wrote(tl)
                mmz = p.step("tensor", lambda e, bd_i=bd_i, zs=zs, c=c, tm=tm: e.matmul(
                    PZ.f32, BDALL.ap[:, 512 * bd_i + 128 * zs:512 * bd_i + 128 * (zs + 1)],
                    C.xnf.ap[:, 2048 * c + 512 * tm:2048 * c + 512 * (tm + 1)],
                    start=True, stop=True),
                    deps=BDALL.rdeps() + C.xnf.rdeps() + PZ.buf.wdeps())
                PZ.buf.wrote(mmz)
                BDALL.read(mmz)
                C.xnf.read(mmz)
                tg = p.step("scalar", lambda e: e.activation(HT.ap[:, 0:512], PZ.f32, AF.Gelu),
                            deps=[mmz] + HT.wdeps())
                PZ.buf.read(tg)
                HT.wrote(tg)
                for j, oc in enumerate(oc_list):
                    pb = POC[j] if opass == 0 else POC[0]
                    mm = p.step("tensor", lambda e, pb=pb, slot=slot, j=j, k=k: e.matmul(
                        pb.f32, slot.ap[:, 128 * j:128 * (j + 1)], HT.ap[:, 0:512],
                        start=(k == 0), stop=(k == 127)),
                        deps=((pb.buf.wdeps() if k == 0 else []) + slot.rdeps() + HT.rdeps()))
                    pb.buf.wrote(mm)
                    slot.read(mm)
                    HT.read(mm)
            for j, oc in enumerate(oc_list):
                pb = POC[j] if opass == 0 else POC[0]
                te = p.step("scalar", lambda e, pb=pb: e.copy(OE.ap[:, 0:512], pb.f32),
                            deps=pb.buf.rdeps() + [pb.buf.w] + OE.wdeps())
                pb.buf.read(te)
                OE.wrote(te)
                for s in range(4):
                    ttr = p.step("tensor", lambda e, s=s: e.transpose(
                        PZ.bf16[:, 0:128], OE.ap[:, 128 * s:128 * (s + 1)], C.ident_bf.ap),
                        deps=OE.rdeps() + PZ.buf.wdeps() + C.ident_bf.rdeps())
                    PZ.buf.wrote(ttr)
                    OE.read(ttr)
                    xsl = C.xt.ap[:, 1024 * (4 * tm + s) + 128 * oc: 1024 * (4 * tm + s) + 128 * (oc + 1)]
                    ta2 = p.step("vector", lambda e, xsl=xsl: e.tensor_tensor(
                        xsl, xsl, PZ.bf16[:, 0:128], ALU.add),
                        deps=[ttr] + C.xt.wdeps())
                    C.xt.wrote(ta2)
                    PZ.buf.read(ta2)


def emit_l2_sincos(C):
    p = C.p
    SC = C.l2_sc
    scf = SC.ap.bitcast(F32)
    tmp = C.sttmp
    k = math.pi / (2.0 * T)
    ti = p.step("gpsimd", lambda e: e.iota(tmp.ap[:, 0:1], [[0, 1]], base=0, channel_multiplier=1,
                                           allow_small_or_imprecise_dtypes=True),
                deps=tmp.wdeps())
    tmp.wrote(ti)
    for tc in range(NTC):
        t1 = p.step("vector", lambda e, tc=tc: e.tensor_scalar(
            tmp.ap[:, 1:2], tmp.ap[:, 0:1], k, (128 * tc + 1) * k, ALU.mult, ALU.add),
            deps=tmp.rdeps() + tmp.wdeps())
        tmp.wrote(t1)
        t2 = p.step("scalar", lambda e, tc=tc: e.activation(scf[:, tc:tc + 1], tmp.ap[:, 1:2], AF.Sin),
                    deps=[t1] + SC.wdeps())
        tmp.read(t2)
        SC.wrote(t2)
        t3 = p.step("vector", lambda e, tc=tc: e.tensor_scalar(
            tmp.ap[:, 1:2], tmp.ap[:, 0:1], -k, math.pi / 2 - (128 * tc + 1) * k, ALU.mult, ALU.add),
            deps=tmp.rdeps() + tmp.wdeps())
        tmp.wrote(t3)
        t4 = p.step("scalar", lambda e, tc=tc: e.activation(scf[:, 16 + tc:17 + tc], tmp.ap[:, 1:2], AF.Sin),
                    deps=[t3] + SC.wdeps())
        tmp.read(t4)
        SC.wrote(t4)
    tn = p.step("vector", lambda e: e.tensor_scalar_mul(scf[:, 32:64], scf[:, 0:32], -1.0),
                deps=SC.rdeps() + SC.wdeps())
    SC.wrote(tn)


def emit_l2_attn(C, b, contrib):
    p = C.p
    dram = C.dram
    ep = C.epoch
    NC = DEV_NCORES
    LR = T // NC
    H = 16

    o = 0
    QS = Buf(C.carve(o, NTC * 1024), "QS", ep); o += NTC * 1024
    AT2 = Buf(C.carve(o, NTC * 1024), "AT2", ep); o += NTC * 1024
    WSA = Buf(C.carve(o, 4096), "WSA", ep); o += 4096
    WSB = Buf(C.carve(o, 4096), "WSB", ep); o += 4096
    KVSB = Buf(C.carve(o, 4160), "KVSB", ep); o += 4160
    KT = Buf(C.carve(o, 512), "KT", ep); o += 512
    KF = Buf(C.carve(o, 1024), "KF", ep); o += 1024
    KF2 = Buf(C.carve(o, 1024), "KF2", ep); o += 1024
    VA = Buf(C.carve(o, 512), "VA", ep); o += 512
    BIAS = Buf(C.carve(o, 2048), "BIAS", ep); o += 2048
    QP = Buf(C.carve(o, 128), "QP", ep); o += 128
    QF = Buf(C.carve(o, 2048), "QF", ep); o += 2048
    PATS = Buf(C.carve(o, 512), "PATS", ep); o += 512
    KVB = Buf(C.carve(o, 260), "KVB", ep); o += 260
    SCc = Buf(C.carve(o, 128), "L2SC", ep); o += 128
    CT = Buf(C.carve(o, 512), "CT", ep); o += 512
    assert o <= C.NSCR, o
    C.l2_sc = SCc
    emit_l2_sincos(C)
    kvsb = KVSB.ap.bitcast(F32)
    biasf = BIAS.ap.bitcast(F32)
    scf = SCc.ap.bitcast(F32)

    wsem = p.new_sem(f"l2w_{b}")
    bsem = p.new_sem(f"l2b_{b}")
    ctsem = p.new_sem(f"l2ct_{b}")

    def load_half(slot, wname, ha):
        src = dram[wname].rearrange("(dc p) o -> p dc o", p=128)[:, :, 512 * ha:512 * (ha + 1)]
        tok = p.dma_group("gpsimd", [lambda e, src=src, slot=slot: e.dma_start(
            out=slot.ap.rearrange("p (dc o) -> p dc o", dc=8), in_=src)], wsem, deps=slot.wdeps())
        slot.wrote(tok)
        return tok

    def load_bias(wname, ha, cols0):
        dst = biasf[0:1, cols0:cols0 + 512]
        tok = p.dma_group("sync", [lambda e, dst=dst, wname=wname, ha=ha: e.dma_start(
            out=dst, in_=dram[wname][0:1, 512 * ha:512 * (ha + 1)])], bsem, deps=BIAS.wdeps())
        BIAS.wrote(tok)
        return tok

    def softmax8(pk, dst_of_hl, dep_tok):
        for hl in range(8):
            psl = pk.f32[:, 64 * hl:64 * (hl + 1)]
            dst = dst_of_hl(hl)
            t1 = p.step("vector", lambda e, psl=psl: e.reduce_max(C.sttmp.ap[:, 2:3], psl, AX.X),
                        deps=[dep_tok] + C.sttmp.wdeps())
            C.sttmp.wrote(t1)
            pk.buf.read(t1)
            t2 = p.step("vector", lambda e: e.tensor_scalar_mul(C.sttmp.ap[:, 3:4], C.sttmp.ap[:, 2:3], -1.0),
                        deps=[t1] + C.sttmp.wdeps())
            C.sttmp.wrote(t2)
            t3 = p.step("scalar", lambda e, psl=psl, dst=dst: e.activation(
                dst[0], psl, AF.Exp, bias=C.sttmp.ap[:, 3:4], accum_out=C.sttmp.ap[:, 4:5]),
                deps=[t2] + dst[1].wdeps())
            C.sttmp.read(t3)
            C.sttmp.wrote(t3)
            pk.buf.read(t3)
            dst[1].wrote(t3)
            t4 = p.step("vector", lambda e: e.reciprocal(C.sttmp.ap[:, 5:6], C.sttmp.ap[:, 4:5]),
                        deps=[t3] + C.sttmp.wdeps())
            C.sttmp.wrote(t4)
            t5 = p.step("vector", lambda e, dst=dst: e.tensor_scalar(
                dst[0], dst[0], C.sttmp.ap[:, 5:6], None, ALU.mult),
                deps=[t4] + dst[1].rdeps() + dst[1].wdeps())
            C.sttmp.read(t5)
            dst[1].wrote(t5)

    def proj_psum(pk, slot, tc, bias_c0, extra=()):
        first = None
        for dc in range(NDC):
            lhs = C.xnf.ap[:, 2048 * dc + 128 * tc:2048 * dc + 128 * (tc + 1)]
            rhs = slot.ap[:, 512 * dc:512 * (dc + 1)]
            mm = p.step("tensor", lambda e, pk=pk, lhs=lhs, rhs=rhs, dc=dc: e.matmul(
                pk.f32, lhs, rhs, start=(dc == 0), stop=False),
                deps=(slot.rdeps() + C.xnf.rdeps() + pk.buf.wdeps() + list(extra)) if dc == 0 else ())
            if first is None:
                first = mm
            slot.read(mm)
            C.xnf.read(mm)
        mm = p.step("tensor", lambda e, pk=pk, bias_c0=bias_c0: e.matmul(
            pk.f32, C.ones_row_f.ap, biasf[0:1, bias_c0:bias_c0 + 512], start=False, stop=True),
            deps=BIAS.rdeps() + C.ones_row_f.rdeps())
        BIAS.read(mm)
        C.ones_row_f.read(mm)
        pk.buf.wrote(mm)
        return mm

    for ha in range(2):
        load_half(WSA, "L2_Wq", ha)
        load_bias("L2_bq", ha, 0)
        for tc in range(NTC):
            pk = C.pb[tc % 2]
            mm = proj_psum(pk, WSA, tc, 0)
            softmax8(pk, lambda hl, tc=tc, ha=ha: (
                QS.ap[:, 1024 * tc + 512 * ha + 64 * hl:1024 * tc + 512 * ha + 64 * (hl + 1)], QS), mm)

    tz = p.step("vector", lambda e: e.memset(kvsb, 0.0), deps=KVSB.wdeps())
    KVSB.wrote(tz)
    ktv = KT.ap.rearrange("p (h x d) -> p h x d", h=8, x=1)
    vav = VA.ap.rearrange("p (h x d) -> p h x d", h=8, x=1)
    kfv = KF.ap.rearrange("p (h two d) -> p h two d", h=8, two=2)
    kf2v = KF2.ap.rearrange("p (h two d) -> p h two d", h=8, two=2)
    for ha in range(2):
        load_half(WSB, "L2_Wk", ha)
        load_half(WSA, "L2_Wv", ha)
        load_bias("L2_bk", ha, 0)
        load_bias("L2_bv", ha, 512)
        for tc in range(NTC):
            pk = C.pb[0]
            pv = C.pb[1]
            mm = proj_psum(pk, WSB, tc, 0)
            softmax8(pk, lambda hl: (KT.ap[:, 64 * hl:64 * (hl + 1)], KT), mm)
            mmv = proj_psum(pv, WSA, tc, 512)
            tva = p.step("scalar", lambda e, pv=pv: e.copy(VA.ap, pv.f32), deps=[mmv] + VA.wdeps())
            VA.wrote(tva)
            pv.buf.read(tva)
            sc = scf[:, tc:tc + 1]
            cc = scf[:, 16 + tc:17 + tc]
            nsc = scf[:, 32 + tc:33 + tc]
            ncc = scf[:, 48 + tc:49 + tc]
            dk = KT.rdeps() + KT.wdeps() + SCc.rdeps()
            f1 = p.step("vector", lambda e, sc=sc: e.tensor_scalar(
                kfv[:, :, 0:1, :], ktv, sc, None, ALU.mult), deps=dk + KF.wdeps())
            f2 = p.step("vector", lambda e, cc=cc: e.tensor_scalar(
                kfv[:, :, 1:2, :], ktv, cc, None, ALU.mult), deps=dk + KF.wdeps())
            KF.wrote(f2)
            f3 = p.step("vector", lambda e, nsc=nsc, sc=sc: e.tensor_scalar(
                kf2v[:, :, 0:1, :], ktv, nsc, sc, ALU.mult, ALU.add), deps=dk + KF2.wdeps())
            f4 = p.step("vector", lambda e, ncc=ncc, cc=cc: e.tensor_scalar(
                kf2v[:, :, 1:2, :], ktv, ncc, cc, ALU.mult, ALU.add), deps=dk + KF2.wdeps())
            KF2.wrote(f4)
            KT.read(f1); KT.read(f2); KT.read(f3); KT.read(f4)
            SCc.read(f4)
            for j in range(16):
                hl, br = j // 2, j % 2
                pkk = C.pb[4 + j % 4]
                srcb = (KF if br == 0 else KF2)
                lhs = srcb.ap[:, 128 * hl:128 * (hl + 1)]
                rhsv = VA.ap[:, 64 * hl:64 * (hl + 1)]
                dd = srcb.rdeps() + VA.rdeps() + [f1, f2, f3, f4, tva] + pkk.buf.wdeps()
                m1 = p.step("tensor", lambda e, pkk=pkk, lhs=lhs, rhsv=rhsv: e.matmul(
                    pkk.f32[:, 0:64], lhs, rhsv, start=True, stop=True), deps=dd)
                m2 = p.step("tensor", lambda e, pkk=pkk, lhs=lhs: e.matmul(
                    pkk.f32[:, 64:65], lhs, C.ones_col_bf.ap, start=True, stop=True),
                    deps=C.ones_col_bf.rdeps())
                srcb.read(m1); srcb.read(m2)
                VA.read(m1)
                C.ones_col_bf.read(m2)
                pkk.buf.wrote(m2)
                dst = kvsb[:, 65 * (16 * ha + j):65 * (16 * ha + j + 1)]
                tf = p.step("vector", lambda e, dst=dst, pkk=pkk: e.tensor_tensor(
                    dst, dst, pkk.f32[:, 0:65], ALU.add),
                    deps=[m2] + KVSB.rdeps() + KVSB.wdeps())
                KVSB.wrote(tf)
                pkk.buf.read(tf)

    for h in range(H):
        kc0 = p.step("vector", lambda e, h=h: e.tensor_copy(KVB.ap[:, 0:65], kvsb[:, 130 * h:130 * h + 65]),
                     deps=KVSB.rdeps() + KVSB.wdeps() + KVB.wdeps())
        kc1 = p.step("vector", lambda e, h=h: e.tensor_copy(KVB.ap[:, 65:130], kvsb[:, 130 * h + 65:130 * h + 130]),
                     deps=KVSB.rdeps() + KVSB.wdeps() + KVB.wdeps())
        KVB.wrote(kc1)
        KVSB.read(kc0); KVSB.read(kc1)
        for br in range(2):
            qsl0 = 64 * (h % 8) + 512 * (h // 8)
            for tc in range(NTC):
                qsl = QS.ap[:, 1024 * tc + qsl0:1024 * tc + qsl0 + 64]
                dq = QS.rdeps() + SCc.rdeps() + QP.wdeps()
                if br == 0:
                    q1 = p.step("vector", lambda e, qsl=qsl, tc=tc: e.tensor_scalar(
                        QP.ap[:, 0:64], qsl, scf[:, tc:tc + 1], None, ALU.mult), deps=dq)
                    q2 = p.step("vector", lambda e, qsl=qsl, tc=tc: e.tensor_scalar(
                        QP.ap[:, 64:128], qsl, scf[:, 16 + tc:17 + tc], None, ALU.mult), deps=dq)
                else:
                    q1 = p.step("vector", lambda e, qsl=qsl, tc=tc: e.tensor_scalar(
                        QP.ap[:, 0:64], qsl, scf[:, 32 + tc:33 + tc], scf[:, tc:tc + 1],
                        ALU.mult, ALU.add), deps=dq)
                    q2 = p.step("vector", lambda e, qsl=qsl, tc=tc: e.tensor_scalar(
                        QP.ap[:, 64:128], qsl, scf[:, 48 + tc:49 + tc], scf[:, 16 + tc:17 + tc],
                        ALU.mult, ALU.add), deps=dq)
                QP.wrote(q2)
                QS.read(q1); QS.read(q2)
                SCc.read(q2)
                pt = C.pb[4 + tc % 2]
                ttr = p.step("tensor", lambda e, pt=pt: e.transpose(pt.bf16[:, 0:128], QP.ap, C.ident_bf.ap),
                             deps=[q1, q2] + pt.buf.wdeps() + C.ident_bf.rdeps())
                pt.buf.wrote(ttr)
                QP.read(ttr)
                te = p.step("scalar", lambda e, pt=pt, tc=tc: e.copy(
                    QF.ap[:, 128 * tc:128 * (tc + 1)], pt.bf16[:, 0:128]),
                    deps=[ttr] + QF.wdeps())
                pt.buf.read(te)
                QF.wrote(te)
            for lt in range(4):
                pa = C.pb[lt % 2]
                kvsl = KVB.ap[:, 65 * br:65 * br + 65]
                mm = p.step("tensor", lambda e, pa=pa, kvsl=kvsl, lt=lt: e.matmul(
                    pa.f32[0:65, 0:512], kvsl, QF.ap[:, 512 * lt:512 * (lt + 1)], start=True, stop=True),
                    deps=KVB.rdeps() + QF.rdeps() + QF.wdeps() + pa.buf.wdeps())
                pa.buf.wrote(mm)
                KVB.read(mm)
                QF.read(mm)
                tev = p.step("scalar", lambda e, pa=pa: e.copy(PATS.ap[0:65, 0:512], pa.f32[0:65, 0:512]),
                             deps=[mm] + PATS.wdeps())
                pa.buf.read(tev)
                PATS.wrote(tev)
                for s4 in range(4):
                    tcg = 4 * lt + s4
                    pt = C.pb[6 + s4 % 2]
                    ttr = p.step("tensor", lambda e, pt=pt, s4=s4: e.transpose(
                        pt.bf16[0:128, 0:65], PATS.ap[0:65, 128 * s4:128 * (s4 + 1)], C.ident_bf.ap[0:65, 0:65]),
                        deps=[tev] + pt.buf.wdeps() + C.ident_bf.rdeps())
                    pt.buf.wrote(ttr)
                    PATS.read(ttr)
                    tz_ = p.step("vector", lambda e, pt=pt: e.reciprocal(C.sttmp.ap[:, 6:7], pt.bf16[:, 64:65]),
                                 deps=[ttr] + C.sttmp.wdeps())
                    C.sttmp.wrote(tz_)
                    pt.buf.read(tz_)
                    asl = AT2.ap[:, 1024 * tcg + 64 * h:1024 * tcg + 64 * (h + 1)]
                    if br == 0:
                        ta = p.step("vector", lambda e, pt=pt, asl=asl: e.tensor_scalar(
                            asl, pt.bf16[:, 0:64], C.sttmp.ap[:, 6:7], None, ALU.mult),
                            deps=[tz_] + AT2.wdeps())
                    else:
                        ta = p.step("vector", lambda e, pt=pt, asl=asl: e.scalar_tensor_tensor(
                            asl, pt.bf16[:, 0:64], C.sttmp.ap[:, 6:7], asl, ALU.mult, ALU.add),
                            deps=[tz_] + AT2.wdeps())
                    AT2.wrote(ta)
                    C.sttmp.read(ta)
                    pt.buf.read(ta)

    toks_ct = []
    for ha in range(2):
        load_half(WSA, "L2_Wo", ha)
        load_bias("L2_bo", ha, 0)
        for tc in range(NTC):
            for dc in range(NDC):
                pt = C.pb[4 + dc % 2]
                src = AT2.ap[:, 1024 * tc + 128 * dc:1024 * tc + 128 * (dc + 1)]
                ttr = p.step("tensor", lambda e, pt=pt, src=src: e.transpose(
                    pt.bf16[:, 0:128], src, C.ident_bf.ap),
                    deps=AT2.rdeps() + AT2.wdeps() + pt.buf.wdeps() + C.ident_bf.rdeps())
                pt.buf.wrote(ttr)
                AT2.read(ttr)
                te = p.step("scalar", lambda e, pt=pt, dc=dc: e.copy(
                    QF.ap[:, 128 * dc:128 * (dc + 1)], pt.bf16[:, 0:128]),
                    deps=[ttr] + QF.wdeps())
                pt.buf.read(te)
                QF.wrote(te)
            po = C.pb[tc % 2]
            mm = None
            for dc in range(NDC):
                lhs = QF.ap[:, 128 * dc:128 * (dc + 1)]
                rhs = WSA.ap[:, 512 * dc:512 * (dc + 1)]
                mm = p.step("tensor", lambda e, po=po, lhs=lhs, rhs=rhs, dc=dc: e.matmul(
                    po.f32, lhs, rhs, start=(dc == 0), stop=False),
                    deps=(QF.rdeps() + QF.wdeps() + WSA.rdeps() + po.buf.wdeps()) if dc == 0 else ())
                QF.read(mm)
                WSA.read(mm)
            mm = p.step("tensor", lambda e, po=po: e.matmul(
                po.f32, C.ones_row_f.ap, biasf[0:1, 0:512], start=False, stop=True),
                deps=BIAS.rdeps() + C.ones_row_f.rdeps())
            BIAS.read(mm)
            C.ones_row_f.read(mm)
            po.buf.wrote(mm)
            tev = p.step("scalar", lambda e, po=po: e.copy(CT.ap, po.f32), deps=[mm] + CT.wdeps())
            po.buf.read(tev)
            CT.wrote(tev)
            d_ = (128 * tc) // LR
            off = (128 * tc) % LR
            dst = contrib[d_, b, off:off + 128, 512 * ha:512 * (ha + 1)]
            tk = p.step("sync", lambda e, dst=dst: e.dma_start(out=dst, in_=CT.ap),
                        deps=CT.rdeps() + CT.wdeps(), dma=True, sem=ctsem)
            CT.read(tk)
            toks_ct.append(tk)
    C.l2_contrib_toks += [toks_ct[-1]]


def emit_l2_scradd(C, b, scr2):
    p = C.p
    ep = C.epoch
    GTs = [Buf(C.carve(40960, 2048), "GT0", ep), Buf(C.carve(43008, 2048), "GT1", ep)]
    gsems = [p.new_sem(f"l2g{i}_{b}") for i in range(2)]
    for tc in range(NTC):
        gt = GTs[tc % 2]
        src = scr2[b, 128 * tc:128 * (tc + 1), :]
        tok = p.dma_group("gpsimd", [lambda e, src=src, gt=gt: e.dma_start(
            out=gt.ap.bitcast(F32), in_=src)], gsems[tc % 2],
            deps=gt.wdeps() + [C.l2_scr_ready])
        gt.wrote(tok)
        xsl = C.xt.ap[:, 1024 * tc:1024 * (tc + 1)]
        ta = p.step("vector", lambda e, xsl=xsl, gt=gt: e.tensor_tensor(
            xsl, xsl, gt.ap.bitcast(F32), ALU.add),
            deps=[tok] + C.xt.wdeps())
        C.xt.wrote(ta)
        gt.read(ta)


def emit_l2_ffn(C, b, h_dram):
    p = C.p
    dram = C.dram
    ep = C.epoch
    o = 0
    WGA = Buf(C.carve(o, 12288), "WGA", ep); o += 12288
    WGB = Buf(C.carve(o, 12288), "WGB", ep); o += 12288
    HB = [Buf(C.carve(o, 3072), "HB0", ep), None]; o += 3072
    HB[1] = Buf(C.carve(o, 3072), "HB1", ep); o += 3072
    GS = Buf(C.carve(o, 1024), "GS", ep); o += 1024
    HT = [Buf(C.carve(o, 512), "HT0", ep), None]; o += 512
    HT[1] = Buf(C.carve(o, 512), "HT1", ep); o += 512
    SCK = Buf(C.carve(o, 2), "SCK", ep); o += 2
    assert o <= 40960, o
    gsf = GS.ap.bitcast(F32)
    sck = SCK.ap.bitcast(F32)
    tk_ = p.step("vector", lambda e: e.memset(sck, 1.0 / (127.0 * 127.0)), deps=SCK.wdeps())
    SCK.wrote(tk_)

    wsem = [p.new_sem(f"l2fw{i}_{b}") for i in range(2)]
    hsp = [p.new_sem(f"l2sp{i}_{b}") for i in range(2)]
    hbsem = [p.new_sem(f"l2hb{i}_{b}") for i in range(2)]

    def load_q(slot, si, q4):
        fns = []
        for wi, wname in enumerate(("L2_wg_t", "L2_wu_t")):
            src = dram[wname].rearrange("(dc p) h -> p dc h", p=128)[:, :, 768 * q4:768 * (q4 + 1)]
            dst = slot.ap[:, 6144 * wi:6144 * (wi + 1)].rearrange("p (dc h) -> p dc h", dc=8)
            fns.append(lambda e, src=src, dst=dst: e.dma_start(out=dst, in_=src))
        tok = p.dma_group("gpsimd", fns, wsem[si], deps=slot.wdeps())
        slot.wrote(tok)
        return tok

    load_q(WGA, 0, 0)
    slots = [WGA, WGB]
    for q4 in range(4):
        slot = slots[q4 % 2]
        if q4 + 1 < 4:
            load_q(slots[(q4 + 1) % 2], (q4 + 1) % 2, q4 + 1)
        for hcl in range(6):
            hcg = 6 * q4 + hcl
            for tt in range(4):
                pg = C.pb[0 + (tt % 2)]
                pu = C.pb[2 + (tt % 2)]
                mmg = mmu = None
                for dc in range(NDC):
                    lhsg = slot.ap[:, 768 * dc + 128 * hcl:768 * dc + 128 * (hcl + 1)]
                    lhsu = slot.ap[:, 6144 + 768 * dc + 128 * hcl:6144 + 768 * dc + 128 * (hcl + 1)]
                    rhs = C.xnf.ap[:, 2048 * dc + 512 * tt:2048 * dc + 512 * (tt + 1)]
                    mmg = p.step("tensor", lambda e, pg=pg, lhsg=lhsg, rhs=rhs, dc=dc: e.matmul(
                        pg.f32, lhsg, rhs, start=(dc == 0), stop=(dc == NDC - 1)),
                        deps=(slot.rdeps() + C.xnf.rdeps() + pg.buf.wdeps()) if dc == 0 else ())
                    mmu = p.step("tensor", lambda e, pu=pu, lhsu=lhsu, rhs=rhs, dc=dc: e.matmul(
                        pu.f32, lhsu, rhs, start=(dc == 0), stop=(dc == NDC - 1)),
                        deps=(slot.rdeps() + C.xnf.rdeps() + pu.buf.wdeps()) if dc == 0 else ())
                    slot.read(mmg); slot.read(mmu)
                    C.xnf.read(mmg); C.xnf.read(mmu)
                pg.buf.wrote(mmg)
                pu.buf.wrote(mmu)
                tg = p.step("scalar", lambda e, pg=pg: e.activation(gsf, pg.f32, AF.Silu, scale=1.0 / 127.0),
                            deps=[mmg] + GS.wdeps())
                GS.wrote(tg)
                pg.buf.read(tg)
                ht = HT[tt % 2]
                th = p.step("vector", lambda e, pu=pu, ht=ht: e.tensor_tensor(ht.ap, gsf, pu.f32, ALU.mult),
                            deps=[tg, mmu] + ht.wdeps())
                ht.wrote(th)
                GS.read(th)
                pu.buf.read(th)
                hdst = h_dram[128 * hcg:128 * (hcg + 1), 512 * tt:512 * (tt + 1)]
                ts_ = p.step("sync", lambda e, hdst=hdst, ht=ht: e.dma_start(out=hdst, in_=ht.ap),
                             deps=ht.rdeps() + ht.wdeps(), dma=True, sem=hsp[tt % 2])
                ht.read(ts_)

    ep2 = p.barrier_all()
    WD = Buf(C.carve(0, 24576), "WD", ep2)
    wdsem = p.new_sem(f"l2wd_{b}")
    src = dram["L2_wd_t"].rearrange("(kc p) o -> p kc o", p=128)
    tok = p.dma_group("gpsimd", [lambda e, src=src: e.dma_start(
        out=WD.ap.rearrange("p (kc o) -> p kc o", kc=24), in_=src)], wdsem, deps=WD.wdeps())
    WD.wrote(tok)
    PW = [C.pb[6], C.pb[7]]
    for tcg in range(NTC):
        hb = HB[tcg % 2]
        hsrc = h_dram[0:3072, 128 * tcg:128 * (tcg + 1)].rearrange("(hc p) t -> p hc t", p=128)
        spill_done = [Tok(s, p.cnt[s]) for s in hsp]
        tl = p.dma_group("sync", [lambda e, hb=hb, hsrc=hsrc: e.dma_start(
            out=hb.ap.rearrange("p (hc t) -> p hc t", hc=24), in_=hsrc)],
            hbsem[tcg % 2], deps=hb.wdeps() + spill_done)
        hb.wrote(tl)
        mm = None
        for hc in range(24):
            lhs = hb.ap[:, 128 * hc:128 * (hc + 1)]
            for nh in range(2):
                rhs = WD.ap[:, 1024 * hc + 512 * nh:1024 * hc + 512 * (nh + 1)]
                pk = PW[nh]
                mm = p.step("tensor", lambda e, pk=pk, lhs=lhs, rhs=rhs, hc=hc: e.matmul(
                    pk.f32, lhs, rhs, start=(hc == 0), stop=(hc == 23)),
                    deps=(WD.rdeps() + [tl] + pk.buf.wdeps()) if hc == 0 else WD.rdeps())
                pk.buf.wrote(mm)
                WD.read(mm)
                hb.read(mm)
        for nh in range(2):
            xsl = C.xt.ap[:, 1024 * tcg + 512 * nh:1024 * tcg + 512 * (nh + 1)]
            ta = p.step("vector", lambda e, xsl=xsl, nh=nh: e.scalar_tensor_tensor(
                xsl, PW[nh].f32, sck, xsl, ALU.mult, ALU.add),
                deps=PW[nh].buf.rdeps() + C.xt.wdeps() + SCK.rdeps())
            C.xt.wrote(ta)
            PW[nh].buf.read(ta)
            SCK.read(ta)


def emit_l3_attn(C, b, sc_row):
    p = C.p
    dram = C.dram
    ep = C.epoch
    o = 0
    GST = Buf(C.carve(o, 8192), "GST", ep); o += 8192
    IWT = Buf(C.carve(o, 8192), "IWT", ep); o += 8192
    WRF = Buf(C.carve(o, 8192), "WRF", ep); o += 8192
    FEATS = Buf(C.carve(o, 16384), "FEATS", ep); o += 16384
    SQ = Buf(C.carve(o, 1024), "SQ", ep); o += 1024
    XISC = Buf(C.carve(o, 16), "XISC", ep); o += 16
    NEG = Buf(C.carve(o, 2048), "NEG", ep); o += 2048
    IWSQ = Buf(C.carve(o, 2048), "IWSQ", ep); o += 2048
    SXR = Buf(C.carve(o, 4096), "SXR", ep); o += 4096
    XST = Buf(C.carve(o, 256), "XST", ep); o += 256
    SCD = Buf(C.carve(o, 2), "SCD", ep); o += 2
    assert o <= C.NSCR, o
    xiscf = XISC.ap.bitcast(F32)
    negf = NEG.ap.bitcast(F32)
    iwsqf = IWSQ.ap.bitcast(F32)
    sxrf = SXR.ap.bitcast(F32)
    xstf = XST.ap.bitcast(F32)
    scd = SCD.ap.bitcast(F32)

    wsem = p.new_sem(f"l3aw_{b}")
    rsem = p.new_sem(f"l3ar_{b}")

    t0 = p.step("vector", lambda e: e.memset(scd, 1.0 / 1024.0), deps=SCD.wdeps())
    SCD.wrote(t0)
    for name, slot in (("L3_gs_t", GST), ("L3_iw_t", IWT)):
        src = dram[name].rearrange("(dc p) r -> p dc r", p=128)
        tok = p.dma_group("gpsimd", [lambda e, src=src, slot=slot: e.dma_start(
            out=slot.ap.rearrange("p (dc r) -> p dc r", dc=8), in_=src)], wsem, deps=slot.wdeps())
        slot.wrote(tok)
    tok = p.dma_group("sync", [lambda e: e.dma_start(out=xiscf, in_=dram["L3_xis_c"])], rsem,
                      deps=XISC.wdeps())
    XISC.wrote(tok)

    PT = C.pb[6]
    tt = p.step("tensor", lambda e: e.transpose(PT.f32[0:8, 0:128], xiscf, C.ident_f32.ap),
                deps=XISC.rdeps() + PT.buf.wdeps() + C.ident_f32.rdeps())
    PT.buf.wrote(tt)
    XISC.read(tt)
    te = p.step("scalar", lambda e: e.copy(xstf[0:8, :], PT.f32[0:8, 0:128]), deps=[tt] + XST.wdeps())
    PT.buf.read(te)
    XST.wrote(te)
    d1 = p.step("sync", lambda e: e.dma_start(out=sc_row[0:1024].rearrange("(a b) -> a b", a=8),
                                              in_=xstf[0:8, :]),
                deps=[te], dma=True, sem=rsem)
    XST.read(d1)
    d2 = p.step("sync", lambda e: e.dma_start(out=negf[0:1, :],
                                              in_=sc_row[0:1024].rearrange("(a b) -> a b", a=1)),
                deps=[d1] + NEG.wdeps(), dma=True, sem=rsem)
    NEG.wrote(d2)
    t1 = p.step("vector", lambda e: e.tensor_scalar_mul(negf[0:1, :], negf[0:1, :], -0.5),
                deps=NEG.rdeps() + NEG.wdeps())
    NEG.wrote(t1)

    P0, P1 = C.pb[2], C.pb[3]
    mml = {0: None, 1: None}
    for dc in range(NDC):
        tq = p.step("scalar", lambda e, dc=dc: e.activation(
            SQ.ap, IWT.ap[:, 1024 * dc:1024 * (dc + 1)], AF.Square),
            deps=IWT.rdeps() + SQ.wdeps())
        SQ.wrote(tq)
        IWT.read(tq)
        for nh in range(2):
            pk = (P0, P1)[nh]
            mm = p.step("tensor", lambda e, pk=pk, nh=nh, dc=dc: e.matmul(
                pk.f32[0:1, 0:512], C.ones_col_bf.ap, SQ.ap[:, 512 * nh:512 * (nh + 1)],
                start=(dc == 0), stop=(dc == NDC - 1)),
                deps=[tq] + (pk.buf.wdeps() + C.ones_col_bf.rdeps() if dc == 0 else []))
            SQ.read(mm)
            C.ones_col_bf.read(mm)
            mml[nh] = mm
            if dc == 0 or dc == NDC - 1:
                pk.buf.wrote(mm)
    for nh in range(2):
        pk = (P0, P1)[nh]
        te2 = p.step("scalar", lambda e, pk=pk, nh=nh: e.copy(iwsqf[0:1, 512 * nh:512 * (nh + 1)],
                                                              pk.f32[0:1, 0:512]),
                     deps=[mml[nh]] + IWSQ.wdeps())
        pk.buf.read(te2)
        IWSQ.wrote(te2)

    for rc in range(NDC):
        for oh in range(2):
            pk = C.pb[oh]
            mm = None
            for dc in range(NDC):
                lhs = GST.ap[:, 1024 * dc + 128 * rc:1024 * dc + 128 * (rc + 1)]
                rhs = IWT.ap[:, 1024 * dc + 512 * oh:1024 * dc + 512 * (oh + 1)]
                mm = p.step("tensor", lambda e, pk=pk, lhs=lhs, rhs=rhs, dc=dc: e.matmul(
                    pk.f32, lhs, rhs, start=(dc == 0), stop=False),
                    deps=(GST.rdeps() + IWT.rdeps() + pk.buf.wdeps()) if dc == 0 else ())
                GST.read(mm)
                IWT.read(mm)
            mm = p.step("tensor", lambda e, pk=pk, rc=rc, oh=oh: e.matmul(
                pk.f32, negf[0:1, 128 * rc:128 * (rc + 1)], iwsqf[0:1, 512 * oh:512 * (oh + 1)],
                start=False, stop=True),
                deps=NEG.rdeps() + IWSQ.rdeps())
            NEG.read(mm)
            IWSQ.read(mm)
            pk.buf.wrote(mm)
            dst = WRF.ap[:, 1024 * rc + 512 * oh:1024 * rc + 512 * (oh + 1)]
            tev = p.step("scalar", lambda e, pk=pk, dst=dst, rc=rc: e.activation(
                dst, pk.f32, AF.Exp, bias=xiscf[:, rc:rc + 1], scale=xiscf[:, rc:rc + 1]),
                deps=[mm] + WRF.wdeps() + XISC.rdeps())
            pk.buf.read(tev)
            WRF.wrote(tev)
            XISC.read(tev)

    tt = p.step("tensor", lambda e: e.transpose(PT.f32[0:16, 0:128], C.stats.ap[:, 48:64], C.ident_f32.ap),
                deps=C.stats.rdeps() + PT.buf.wdeps() + C.ident_f32.rdeps())
    PT.buf.wrote(tt)
    C.stats.read(tt)
    te = p.step("scalar", lambda e: e.copy(xstf[0:16, :], PT.f32[0:16, 0:128]),
                deps=[tt] + XST.wdeps())
    PT.buf.read(te)
    XST.wrote(te)
    d3 = p.step("sync", lambda e: e.dma_start(out=sc_row[0:2048].rearrange("(a b) -> a b", a=16),
                                              in_=xstf[0:16, :]),
                deps=[te], dma=True, sem=rsem)
    XST.read(d3)
    d4 = p.step("sync", lambda e: e.dma_start(out=sxrf[0:1, :],
                                              in_=sc_row[0:2048].rearrange("(a b) -> a b", a=1)),
                deps=[d3] + SXR.wdeps(), dma=True, sem=rsem)
    SXR.wrote(d4)

    for rc in range(NDC):
        for tt_ in range(4):
            pk = C.pb[tt_ % 2]
            mm = None
            for dc in range(NDC):
                lhs = GST.ap[:, 1024 * dc + 128 * rc:1024 * dc + 128 * (rc + 1)]
                rhs = C.xnf.ap[:, 2048 * dc + 512 * tt_:2048 * dc + 512 * (tt_ + 1)]
                mm = p.step("tensor", lambda e, pk=pk, lhs=lhs, rhs=rhs, dc=dc: e.matmul(
                    pk.f32, lhs, rhs, start=(dc == 0), stop=False),
                    deps=(GST.rdeps() + C.xnf.rdeps() + pk.buf.wdeps()) if dc == 0 else ())
                GST.read(mm)
                C.xnf.read(mm)
            mm = p.step("tensor", lambda e, pk=pk, rc=rc, tt_=tt_: e.matmul(
                pk.f32, negf[0:1, 128 * rc:128 * (rc + 1)], sxrf[0:1, 512 * tt_:512 * (tt_ + 1)],
                start=False, stop=True),
                deps=NEG.rdeps() + SXR.rdeps())
            NEG.read(mm)
            SXR.read(mm)
            pk.buf.wrote(mm)
            dst = FEATS.ap[:, 2048 * rc + 512 * tt_:2048 * rc + 512 * (tt_ + 1)]
            tev = p.step("scalar", lambda e, pk=pk, dst=dst, rc=rc: e.activation(
                dst, pk.f32, AF.Exp, bias=xiscf[:, rc:rc + 1], scale=xiscf[:, rc:rc + 1]),
                deps=[mm] + FEATS.wdeps() + XISC.rdeps())
            pk.buf.read(tev)
            FEATS.wrote(tev)
            XISC.read(tev)

    for tc in range(NTC):
        for oh in range(2):
            pk = C.pb[4 + oh]
            mm = None
            for rc in range(NDC):
                lhs = FEATS.ap[:, 2048 * rc + 128 * tc:2048 * rc + 128 * (tc + 1)]
                rhs = WRF.ap[:, 1024 * rc + 512 * oh:1024 * rc + 512 * (oh + 1)]
                mm = p.step("tensor", lambda e, pk=pk, lhs=lhs, rhs=rhs, rc=rc: e.matmul(
                    pk.f32, lhs, rhs, start=(rc == 0), stop=(rc == NDC - 1)),
                    deps=(FEATS.rdeps() + WRF.rdeps() + pk.buf.wdeps()) if rc == 0 else ())
                FEATS.read(mm)
                WRF.read(mm)
            pk.buf.wrote(mm)
            xsl = C.xt.ap[:, 1024 * tc + 512 * oh:1024 * tc + 512 * (oh + 1)]
            ta = p.step("vector", lambda e, xsl=xsl, pk=pk: e.scalar_tensor_tensor(
                xsl, pk.f32, scd, xsl, ALU.mult, ALU.add),
                deps=[mm] + C.xt.wdeps() + SCD.rdeps())
            C.xt.wrote(ta)
            pk.buf.read(ta)
            SCD.read(ta)


def emit_l3_gates(C, b):
    p = C.p
    dram = C.dram
    ep = C.epoch
    CAP = 513
    NH3 = 32
    oP = 50688
    RTRT = Buf(C.carve(oP, 128), "RTRT", ep); oP += 128
    SCA = Buf(C.carve(oP, 256), "SCA", ep); oP += 256
    SCB = Buf(C.carve(oP, 256), "SCB", ep); oP += 256
    GSC = Buf(C.carve(oP, 256), "GSC", ep); oP += 256
    MSK = Buf(C.carve(oP, 256), "MSK", ep); oP += 256
    GM = Buf(C.carve(oP, 16), "GM", ep); oP += 16
    GMT = Buf(C.carve(oP, 256), "GMT", ep); oP += 256
    BIS = Buf(C.carve(oP, 64), "BIS", ep); oP += 64
    B1C = Buf(C.carve(oP, 512), "B1C", ep); oP += 512
    B2R = Buf(C.carve(oP, 2048), "B2R", ep); oP += 2048
    XNTF = Buf(C.carve(0, 2048), "XNTF3", ep)
    assert oP <= C.NSCR, oP
    C.l3g = (GSC, B1C, B2R)
    rtf = RTRT.ap.bitcast(F32)
    scaf = SCA.ap.bitcast(F32)
    scbf = SCB.ap.bitcast(F32)
    gscf = GSC.ap.bitcast(F32)
    mskf = MSK.ap.bitcast(F32)
    gmf = GM.ap.bitcast(F32)
    gmtf = GMT.ap.bitcast(F32)
    bis = BIS.ap.bitcast(F32)
    b1c = B1C.ap.bitcast(F32)
    b2r = B2R.ap.bitcast(F32)
    xntff = XNTF.ap.bitcast(F32)

    rsem = p.new_sem(f"l3r_{b}")
    tok = p.dma_group("sync", [lambda e: e.dma_start(
        out=rtf.rearrange("p (dc e) -> p dc e", dc=8),
        in_=dram["L3_rt_t"].rearrange("(dc p) e -> p dc e", p=128))], rsem, deps=RTRT.wdeps())
    RTRT.wrote(tok)
    tok = p.dma_group("sync", [lambda e: e.dma_start(
        out=b1c.rearrange("p (hc e) -> p hc e", hc=NH3),
        in_=dram["L3_b1_t"].rearrange("(hc p) e -> p hc e", p=128))], rsem, deps=B1C.wdeps())
    B1C.wrote(tok)

    PD = C.pb[0]
    PTR = C.pb[6]
    for tc in range(NTC):
        txn = p.step("vector", lambda e, tc=tc: e.tensor_scalar(
            xntff, C.xt.ap[:, 1024 * tc:1024 * (tc + 1)],
            C.stats.ap[:, tc:tc + 1], C.stats.ap[:, 32 + tc:33 + tc],
            ALU.subtract, ALU.mult),
            deps=C.stats.rdeps() + C.xt.rdeps() + XNTF.wdeps())
        XNTF.wrote(txn)
        C.stats.read(txn)
        C.xt.read(txn)
        mm = None
        for dc in range(NDC):
            ttr = p.step("tensor", lambda e, dc=dc: e.transpose(
                PTR.f32[:, 0:128], xntff[:, 128 * dc:128 * (dc + 1)], C.ident_f32.ap),
                deps=[txn] + PTR.buf.wdeps() + C.ident_f32.rdeps())
            PTR.buf.wrote(ttr)
            XNTF.read(ttr)
            tev = p.step("scalar", lambda e, dc=dc: e.copy(xntff[:, 128 * dc:128 * (dc + 1)],
                                                           PTR.f32[:, 0:128]),
                         deps=[ttr] + XNTF.wdeps() + XNTF.rdeps())
            PTR.buf.read(tev)
            XNTF.wrote(tev)
            mm = p.step("tensor", lambda e, dc=dc: e.matmul(
                PD.f32[:, 0:8], xntff[:, 128 * dc:128 * (dc + 1)], rtf[:, 8 * dc:8 * (dc + 1)],
                start=(dc == 0), stop=(dc == NDC - 1)),
                deps=[tev] + RTRT.rdeps() + (PD.buf.wdeps() if dc == 0 else []))
            XNTF.read(mm)
            RTRT.read(mm)
        PD.buf.wrote(mm)
        t1 = p.step("vector", lambda e: e.reduce_max(C.sttmp.ap[:, 2:3], PD.f32[:, 0:8], AX.X),
                    deps=[mm] + C.sttmp.wdeps())
        C.sttmp.wrote(t1)
        PD.buf.read(t1)
        t2 = p.step("vector", lambda e: e.tensor_scalar_mul(C.sttmp.ap[:, 3:4], C.sttmp.ap[:, 2:3], -1.0),
                    deps=[t1] + C.sttmp.wdeps())
        C.sttmp.wrote(t2)
        dst = scaf[:, 8 * tc:8 * (tc + 1)]
        t3 = p.step("scalar", lambda e, dst=dst: e.activation(
            dst, PD.f32[:, 0:8], AF.Exp, bias=C.sttmp.ap[:, 3:4], accum_out=C.sttmp.ap[:, 4:5]),
            deps=[t2] + SCA.wdeps())
        C.sttmp.read(t3)
        C.sttmp.wrote(t3)
        PD.buf.read(t3)
        SCA.wrote(t3)
        t4 = p.step("vector", lambda e: e.reciprocal(C.sttmp.ap[:, 5:6], C.sttmp.ap[:, 4:5]),
                    deps=[t3] + C.sttmp.wdeps())
        C.sttmp.wrote(t4)
        t5 = p.step("vector", lambda e, dst=dst: e.tensor_scalar(
            dst, dst, C.sttmp.ap[:, 5:6], None, ALU.mult),
            deps=[t4] + SCA.rdeps() + SCA.wdeps())
        C.sttmp.read(t5)
        SCA.wrote(t5)

    tt = p.step("tensor", lambda e: e.transpose(PTR.f32[:, 0:128], scaf[:, 0:128], C.ident_f32.ap),
                deps=SCA.rdeps() + SCA.wdeps() + PTR.buf.wdeps() + C.ident_f32.rdeps())
    PTR.buf.wrote(tt)
    SCA.read(tt)
    te = p.step("scalar", lambda e: e.copy(scbf[:, 0:128], PTR.f32[:, 0:128]), deps=[tt] + SCB.wdeps())
    PTR.buf.read(te)
    SCB.wrote(te)

    gsem = p.new_sem(f"l3g_{b}")
    tg = None
    for k16 in range(16):
        dst = gmf[8 * k16:8 * (k16 + 1), :]
        tg = p.step("sync", lambda e, dst=dst: e.dma_start(out=dst, in_=C.ident_f32.ap[0:8, 0:8]),
                    deps=(C.ident_f32.rdeps() + GM.wdeps()) if k16 == 0 else [], dma=True, sem=gsem)
        C.ident_f32.read(tg)
    GM.wrote(tg)
    tt = p.step("tensor", lambda e: e.transpose(PTR.f32[0:8, 0:128], gmf, C.ident_f32.ap),
                deps=GM.rdeps() + GM.wdeps() + PTR.buf.wdeps() + C.ident_f32.rdeps())
    PTR.buf.wrote(tt)
    GM.read(tt)
    te = p.step("scalar", lambda e: e.copy(gmtf[0:8, 0:128], PTR.f32[0:8, 0:128]), deps=[tt] + GMT.wdeps())
    PTR.buf.read(te)
    GMT.wrote(te)

    tz = p.step("vector", lambda e: e.memset(bis[0:8, 0:1], 0.0), deps=BIS.wdeps())
    BIS.wrote(tz)
    tz = p.step("vector", lambda e: e.memset(bis[0:8, 1:2], 1.0), deps=BIS.wdeps())
    BIS.wrote(tz)
    PC = C.pb[1]

    def theta_bcast(src_col):
        mm = p.step("tensor", lambda e, src_col=src_col: e.matmul(
            PC.f32[:, 0:1], gmtf[0:8, 0:128], bis[0:8, src_col:src_col + 1], start=True, stop=True),
            deps=GMT.rdeps() + BIS.rdeps() + BIS.wdeps() + PC.buf.wdeps())
        PC.buf.wrote(mm)
        GMT.read(mm)
        BIS.read(mm)
        tc_ = p.step("vector", lambda e: e.tensor_copy(bis[:, 15:16], PC.f32[:, 0:1]),
                     deps=[mm] + BIS.wdeps())
        BIS.wrote(tc_)
        PC.buf.read(tc_)
        return tc_

    for it in range(46):
        ta = p.step("vector", lambda e: e.tensor_tensor(bis[0:8, 2:3], bis[0:8, 0:1], bis[0:8, 1:2],
                                                        ALU.add), deps=BIS.rdeps() + BIS.wdeps())
        BIS.wrote(ta)
        tb = p.step("vector", lambda e: e.tensor_scalar_mul(bis[0:8, 2:3], bis[0:8, 2:3], 0.5),
                    deps=[ta] + BIS.wdeps())
        BIS.wrote(tb)
        theta_bcast(2)
        tm = p.step("vector", lambda e: e.tensor_scalar(mskf, scbf, bis[:, 15:16], None, ALU.is_gt),
                    deps=SCB.rdeps() + BIS.rdeps() + MSK.wdeps())
        MSK.wrote(tm)
        SCB.read(tm)
        BIS.read(tm)
        tr = p.step("vector", lambda e: e.reduce_sum(bis[:, 14:15], mskf, AX.X),
                    deps=[tm] + BIS.wdeps())
        BIS.wrote(tr)
        MSK.read(tr)
        mm = p.step("tensor", lambda e: e.matmul(PC.f32[0:8, 0:1], gmf, bis[:, 14:15],
                                                 start=True, stop=True),
                    deps=[tr] + GM.rdeps() + PC.buf.wdeps())
        PC.buf.wrote(mm)
        GM.read(mm)
        BIS.read(mm)
        tc_ = p.step("vector", lambda e: e.tensor_copy(bis[0:8, 6:7], PC.f32[0:8, 0:1]),
                     deps=[mm] + BIS.wdeps())
        BIS.wrote(tc_)
        PC.buf.read(tc_)
        ts_ = p.step("vector", lambda e: e.tensor_scalar(bis[0:8, 4:5], bis[0:8, 6:7], float(CAP), None,
                                                         ALU.is_ge), deps=[tc_] + BIS.wdeps())
        BIS.wrote(ts_)
        tn = p.step("vector", lambda e: e.tensor_scalar(bis[0:8, 5:6], bis[0:8, 4:5], -1.0, 1.0,
                                                        ALU.mult, ALU.add), deps=[ts_] + BIS.wdeps())
        BIS.wrote(tn)
        td = p.step("vector", lambda e: e.tensor_tensor(bis[0:8, 3:4], bis[0:8, 2:3], bis[0:8, 0:1],
                                                        ALU.subtract), deps=[tn] + BIS.wdeps())
        BIS.wrote(td)
        tl = p.step("vector", lambda e: e.scalar_tensor_tensor(
            bis[0:8, 0:1], bis[0:8, 3:4], bis[0:8, 4:5], bis[0:8, 0:1], ALU.mult, ALU.add),
            deps=[td] + BIS.wdeps())
        BIS.wrote(tl)
        td2 = p.step("vector", lambda e: e.tensor_tensor(bis[0:8, 3:4], bis[0:8, 2:3], bis[0:8, 1:2],
                                                         ALU.subtract), deps=[tl] + BIS.wdeps())
        BIS.wrote(td2)
        th = p.step("vector", lambda e: e.scalar_tensor_tensor(
            bis[0:8, 1:2], bis[0:8, 3:4], bis[0:8, 5:6], bis[0:8, 1:2], ALU.mult, ALU.add),
            deps=[td2] + BIS.wdeps())
        BIS.wrote(th)

    theta_bcast(0)
    tm = p.step("vector", lambda e: e.tensor_scalar(mskf, scbf, bis[:, 15:16], None, ALU.is_gt),
                deps=SCB.rdeps() + BIS.rdeps() + MSK.wdeps())
    MSK.wrote(tm)
    SCB.read(tm)
    BIS.read(tm)
    tq = p.step("vector", lambda e: e.tensor_tensor(mskf, mskf, scbf, ALU.mult),
                deps=[tm] + MSK.wdeps())
    MSK.wrote(tq)
    SCB.read(tq)
    tt = p.step("tensor", lambda e: e.transpose(PTR.f32[:, 0:128], mskf, C.ident_f32.ap),
                deps=MSK.rdeps() + MSK.wdeps() + PTR.buf.wdeps() + C.ident_f32.rdeps())
    PTR.buf.wrote(tt)
    MSK.read(tt)
    te = p.step("scalar", lambda e: e.copy(gscf, PTR.f32[:, 0:128]), deps=[tt] + GSC.wdeps())
    PTR.buf.read(te)
    GSC.wrote(te)

    if DEV_DBG:
        dsem = p.new_sem(f"l3dbg_{b}")
        dbg = C.dram.get("dbg")
        t1_ = p.step("sync", lambda e: e.dma_start(out=dbg[0:128, 0:128], in_=gscf),
                     deps=GSC.rdeps() + GSC.wdeps(), dma=True, sem=dsem)
        GSC.read(t1_)
        t2_ = p.step("sync", lambda e: e.dma_start(out=dbg[0:128, 128:256], in_=scaf),
                     deps=SCA.rdeps() + SCA.wdeps() + [t1_], dma=True, sem=dsem)
        SCA.read(t2_)
        t3_ = p.step("sync", lambda e: e.dma_start(out=dbg[0:128, 256:384], in_=scbf),
                     deps=SCB.rdeps() + SCB.wdeps() + [t2_], dma=True, sem=dsem)
        SCB.read(t3_)
        t4_ = p.step("sync", lambda e: e.dma_start(out=dbg[0:8, 384:386], in_=bis[0:8, 0:2]),
                     deps=BIS.rdeps() + BIS.wdeps() + [t3_], dma=True, sem=dsem)

        BIS.read(t4_)
        t5_ = p.step("sync", lambda e: e.dma_start(out=dbg[0:128, 416:424], in_=gmf),
                     deps=GM.rdeps() + GM.wdeps() + [t4_], dma=True, sem=dsem)
        GM.read(t5_)


def emit_l3_experts(C, b, h_dram):
    p = C.p
    dram = C.dram
    ep = C.epoch
    NH3 = 32
    GSC, B1C, B2R = C.l3g
    gscf = GSC.ap.bitcast(F32)
    b1c = B1C.ap.bitcast(F32)
    b2r = B2R.ap.bitcast(F32)
    W2 = Buf(C.carve(0, 32768), "W2e", ep)
    W1S = [Buf(C.carve(32768, 4096), "W1A", ep), Buf(C.carve(36864, 4096), "W1B", ep)]
    HB = [Buf(C.carve(40960, 4096), "HB0", ep), Buf(C.carve(45056, 4096), "HB1", ep)]
    HT = [Buf(C.carve(49152, 512), "HT0", ep), Buf(C.carve(49664, 512), "HT1", ep)]
    w2sem = p.new_sem(f"l3w2_{b}")
    w1sem = [p.new_sem(f"l3w1{i}_{b}") for i in range(2)]
    bsem = p.new_sem(f"l3b_{b}")
    hsp = [p.new_sem(f"l3sp{i}_{b}") for i in range(2)]
    hbsem = [p.new_sem(f"l3hb{i}_{b}") for i in range(2)]

    def load_w1(slot, si, e_i, q8):
        src = dram["L3_w1"][e_i].rearrange("(dc p) h -> p dc h", p=128)[:, :, 512 * q8:512 * (q8 + 1)]
        tok = p.dma_group("gpsimd", [lambda e, src=src, slot=slot: e.dma_start(
            out=slot.ap.rearrange("p (dc h) -> p dc h", dc=8), in_=src)], w1sem[si], deps=slot.wdeps())
        slot.wrote(tok)
        return tok

    def load_w2(e_i):
        src = dram["L3_w2"][e_i].rearrange("(hc p) o -> p hc o", p=128)
        tok = p.dma_group("gpsimd", [lambda e, src=src: e.dma_start(
            out=W2.ap.rearrange("p (hc o) -> p hc o", hc=NH3), in_=src)], w2sem, deps=W2.wdeps())
        W2.wrote(tok)
        return tok

    PW = [C.pb[4], C.pb[5]]
    for e_i in range(8):
        load_w2(e_i)
        tokb = p.dma_group("sync", [lambda e, e_i=e_i: e.dma_start(
            out=b2r[0:1, :], in_=dram["L3_b2"][e_i:e_i + 1, :])], bsem, deps=B2R.wdeps())
        B2R.wrote(tokb)
        load_w1(W1S[0], 0, e_i, 0)
        for q8 in range(8):
            slot = W1S[q8 % 2]
            if q8 + 1 < 8:
                load_w1(W1S[(q8 + 1) % 2], (q8 + 1) % 2, e_i, q8 + 1)
            for hcl in range(4):
                hcg = 4 * q8 + hcl
                bcol = 8 * hcg + e_i
                for tt_ in range(4):
                    pk = C.pb[(4 * hcl + tt_) % 4]
                    mm = None
                    for dc in range(NDC):
                        lhs = slot.ap[:, 512 * dc + 128 * hcl:512 * dc + 128 * (hcl + 1)]
                        rhs = C.xnf.ap[:, 2048 * dc + 512 * tt_:2048 * dc + 512 * (tt_ + 1)]
                        mm = p.step("tensor", lambda e, pk=pk, lhs=lhs, rhs=rhs, dc=dc: e.matmul(
                            pk.f32, lhs, rhs, start=(dc == 0), stop=(dc == NDC - 1)),
                            deps=(slot.rdeps() + C.xnf.rdeps() + pk.buf.wdeps()) if dc == 0 else ())
                        slot.read(mm)
                        C.xnf.read(mm)
                    pk.buf.wrote(mm)
                    ht = HT[tt_ % 2]
                    tg = p.step("scalar", lambda e, pk=pk, ht=ht, bcol=bcol: e.activation(
                        ht.ap, pk.f32, AF.Gelu, bias=b1c[:, bcol:bcol + 1]),
                        deps=[mm] + ht.wdeps() + B1C.rdeps())
                    ht.wrote(tg)
                    pk.buf.read(tg)
                    B1C.read(tg)
                    hdst = h_dram[128 * hcg:128 * (hcg + 1), 512 * tt_:512 * (tt_ + 1)]
                    ts_ = p.step("sync", lambda e, hdst=hdst, ht=ht: e.dma_start(out=hdst, in_=ht.ap),
                                 deps=ht.rdeps() + ht.wdeps(), dma=True, sem=hsp[tt_ % 2])
                    ht.read(ts_)
        for tcg in range(NTC):
            hb = HB[tcg % 2]
            hsrc = h_dram[0:4096, 128 * tcg:128 * (tcg + 1)].rearrange("(hc p) t -> p hc t", p=128)
            spill_done = [Tok(s, p.cnt[s]) for s in hsp]
            tl2 = p.dma_group("sync", [lambda e, hb=hb, hsrc=hsrc: e.dma_start(
                out=hb.ap.rearrange("p (hc t) -> p hc t", hc=NH3), in_=hsrc)],
                hbsem[tcg % 2], deps=hb.wdeps() + spill_done)
            hb.wrote(tl2)
            mm = {0: None, 1: None}
            for hc in range(NH3):
                lhs = hb.ap[:, 128 * hc:128 * (hc + 1)]
                for nh in range(2):
                    rhs = W2.ap[:, 1024 * hc + 512 * nh:1024 * hc + 512 * (nh + 1)]
                    pk = PW[nh]
                    m_ = p.step("tensor", lambda e, pk=pk, lhs=lhs, rhs=rhs, hc=hc: e.matmul(
                        pk.f32, lhs, rhs, start=(hc == 0), stop=False),
                        deps=(W2.rdeps() + [tl2] + pk.buf.wdeps()) if hc == 0 else W2.rdeps())
                    W2.read(m_)
                    hb.read(m_)
                    mm[nh] = m_
            for nh in range(2):
                pk = PW[nh]
                m_ = p.step("tensor", lambda e, pk=pk, nh=nh: e.matmul(
                    pk.f32, C.ones_row_f.ap, b2r[0:1, 512 * nh:512 * (nh + 1)], start=False, stop=True),
                    deps=B2R.rdeps() + C.ones_row_f.rdeps())
                B2R.read(m_)
                C.ones_row_f.read(m_)
                pk.buf.wrote(m_)
                gcol = gscf[:, 8 * tcg + e_i:8 * tcg + e_i + 1]
                xsl = C.xt.ap[:, 1024 * tcg + 512 * nh:1024 * tcg + 512 * (nh + 1)]
                ta = p.step("vector", lambda e, xsl=xsl, pk=pk, gcol=gcol: e.scalar_tensor_tensor(
                    xsl, pk.f32, gcol, xsl, ALU.mult, ALU.add),
                    deps=[m_] + C.xt.wdeps() + GSC.rdeps())
                C.xt.wrote(ta)
                pk.buf.read(ta)
                GSC.read(ta)


def build_program(layers):
    nc = bass.Bass("TRN2", target_bir_lowering=False, debug=False)
    C = Ctx()
    C.nc = nc
    p = Prog()
    C.p = p
    dram = {}
    C.dram = dram

    def din(name, shape, dt=F32):
        dram[name] = nc.dram_tensor(name, list(shape), dt, kind="ExternalInput").ap()
        return dram[name]

    x_in = din("x", [B_PER, T, D])
    x_out = nc.dram_tensor("y", [B_PER, T, D], F32, kind="ExternalOutput").ap()

    if 0 in layers:
        din("L0_Wqkv_t", [D, 3 * D])
        din("L0_Wout_t", [D, D])
        din("L0_cent_t", [D, 8])
        din("L0_Wg", [8, D, 2736])
        din("L0_bg_t", [2816, 8])
        din("L0_W1", [8, D, 2736])
        din("L0_b1_t", [2816, 8])
        din("L0_W2", [8, 2736, D])
        din("L0_b2", [8, D])
    if 1 in layers:
        din("L1_wcq", [D, 96]); din("L1_gq", [1, 96]); din("L1_gkv", [1, 32])
        din("L1_wdqn", [96, 512]); din("L1_wdqr", [96, 1024])
        din("L1_wckv", [D, 32]); din("L1_wdkn", [32, 512]); din("L1_wdv", [32, 512])
        din("L1_wkr", [D, 64]); din("L1_wimp", [128, 8])
        din("L1_wselk_n", [D, 512]); din("L1_wselk_r", [D, 1024]); din("L1_wselv", [D, 512])
        din("L1_wink_n", [D, 512]); din("L1_wink_r", [D, 1024]); din("L1_winv", [D, 512])
        din("L1_wgate", [D, 3]); din("L1_wproj", [512, D])
        din("L1_rmT", [4, 64, 16, 16]); din("L1_rupT", [4, 64, 16, 64])
        din("L1_wcproj_t", [16384, D])
    contrib = gathered = scr2 = xsave = sc_row = None
    if 2 in layers:
        for nm in ("Wq", "Wk", "Wv", "Wo"):
            din(f"L2_{nm}", [D, D])
        for nm in ("bq", "bk", "bv", "bo"):
            din(f"L2_{nm}", [1, D])
        din("L2_wg_t", [D, 3072], I32)
        din("L2_wu_t", [D, 3072], I32)
        din("L2_wd_t", [3072, D], I32)
        LR = T // DEV_NCORES
        contrib = nc.dram_tensor("l2cont", [DEV_NCORES, B_PER, LR, D], BF16).ap()
        gathered = nc.dram_tensor("l2gath", [DEV_NCORES, B_PER, LR, D], BF16).ap()
        scr2 = nc.dram_tensor("l2scr", [B_PER, T, D], BF16).ap()
        xsave = nc.dram_tensor("xsave", [B_PER, T, D], F32).ap()
    if 3 in layers:
        din("L3_gs_t", [D, 1024]); din("L3_iw_t", [D, 1024]); din("L3_xis_c", [128, 8])
        din("L3_rt_t", [D, 8]); din("L3_w1", [8, D, 4096]); din("L3_b1_t", [4096, 8])
        din("L3_w2", [8, 4096, D]); din("L3_b2", [8, D])
        sc_row = nc.dram_tensor("sc_row", [4096], F32).ap()
    if DEV_DBG:
        dram["dbg"] = nc.dram_tensor("dbg", [128, 2048], F32, kind="ExternalOutput").ap()
    h_dram = nc.dram_tensor("h_dram", [4096, T], BF16).ap()

    stack = ExitStack()

    def sb(name, shape, dt):
        return stack.enter_context(nc.sbuf_tensor(name, list(shape), dt))

    C.xt = Buf(sb("XT", [128, NTC * 1024], F32)[:], "XT")
    C.xnf = Buf(sb("XNF", [128, NDC * 2048], BF16)[:], "XNF")
    C.stats = Buf(sb("STATS", [128, 80], F32)[:], "STATS")
    C.sttmp = Buf(sb("STTMP", [128, 12], F32)[:], "STTMP")
    C.ident_bf = Buf(sb("IDENT", [128, 128], BF16)[:], "IDENT")
    C.ident_f32 = Buf(sb("IDENTF", [128, 128], F32)[:], "IDENTF")
    C.ones_col_bf = Buf(sb("ONESCB", [128, 1], BF16)[:], "ONESCB")
    C.ones_col_f = Buf(sb("ONESCF", [128, 1], F32)[:], "ONESCF")
    C.ones_row_f = Buf(sb("ONESRF", [1, 128], F32)[:], "ONESRF")
    C.mask0 = Buf(sb("MASK0", [128, 256], BF16)[:], "MASK0")
    C.mask1 = Buf(sb("MASK1", [128, 256], BF16)[:], "MASK1")
    C.iotat = Buf(sb("IOTAT", [128, 256], F32)[:], "IOTAT")
    C.rep4 = Buf(sb("REP4", [32, 128], BF16)[:], "REP4")
    C.nshift = Buf(sb("NSHIFT", [128, 1], F32)[:], "NSHIFT")
    C.sut16 = Buf(sb("SUT16", [16, 16], F32)[:], "SUT16")

    C.NSCR = 54 * 1024 - 128
    C.RSV0 = C.NSCR - 4 * 1024
    scr = sb("SCR", [128, C.NSCR], BF16)
    scr_ap = scr[:]
    C.xnt = Buf(scr_ap[:, 0:NTC * 1024], "XNT")
    C.SCR0 = NTC * 1024

    def carve(off, n):
        assert off + n <= C.NSCR, (off, n)
        return scr_ap[:, off:off + n]

    C.carve = carve

    class Bank:
        pass

    C.pb = []
    for i in range(8):
        t = stack.enter_context(nc.psum_tensor(f"PBK{i}", [128, 512], F32))
        bk = Bank()
        bk.f32 = t[:]
        bk.bf16 = t[:].bitcast(BF16)
        bk.buf = Buf(None, f"psum{i}")
        C.pb.append(bk)

    ti = p.step("gpsimd", lambda e: e.iota(C.iotat.ap, [[1, 256]], base=0, channel_multiplier=0,
                                           allow_small_or_imprecise_dtypes=True))
    C.iotat.wrote(ti)
    tc_ = p.step("gpsimd", lambda e: e.iota(C.stats.ap[:, 64:65], [[0, 1]], base=0, channel_multiplier=1,
                                            allow_small_or_imprecise_dtypes=True))
    tid = p.step("vector", lambda e: e.tensor_scalar(C.ident_f32.ap, C.iotat.ap[:, 0:128],
                                                     C.stats.ap[:, 64:65], None, ALU.is_equal),
                 deps=[ti, tc_])
    C.ident_f32.wrote(tid)
    tidb = p.step("vector", lambda e: e.tensor_copy(C.ident_bf.ap, C.ident_f32.ap), deps=[tid])
    C.ident_bf.wrote(tidb)
    tm0 = p.step("vector", lambda e: e.tensor_scalar(C.mask0.ap, C.iotat.ap,
                                                     C.stats.ap[:, 64:65], None, ALU.is_ge), deps=[ti, tc_])
    C.mask0.wrote(tm0)
    tm1c = p.step("vector", lambda e: e.tensor_scalar_add(C.stats.ap[:, 65:66], C.stats.ap[:, 64:65], 128.0),
                  deps=[tc_])
    tm1 = p.step("vector", lambda e: e.tensor_scalar(C.mask1.ap, C.iotat.ap,
                                                     C.stats.ap[:, 65:66], None, ALU.is_ge), deps=[ti, tm1c])
    C.mask1.wrote(tm1)
    t_o1 = p.step("vector", lambda e: e.memset(C.ones_col_bf.ap, 1.0))
    C.ones_col_bf.wrote(t_o1)
    t_o1f = p.step("vector", lambda e: e.memset(C.ones_col_f.ap, 1.0))
    C.ones_col_f.wrote(t_o1f)
    t_ns = p.step("vector", lambda e: e.memset(C.nshift.ap, -12.0))
    C.nshift.wrote(t_ns)
    for j4 in range(4):
        tr4 = p.step("vector", lambda e, j4=j4: e.tensor_copy(C.rep4.ap[0:32, 32 * j4:32 * (j4 + 1)],
                                                              C.ident_bf.ap[0:32, 0:32]),
                     deps=C.ident_bf.rdeps() + (C.rep4.wdeps() if j4 == 0 else []))
        C.ident_bf.read(tr4)
    C.rep4.wrote(p.barrier_tok("vector"))
    ts16 = p.step("vector", lambda e: e.tensor_scalar(C.sut16.ap, C.iotat.ap[0:16, 0:16],
                                                      C.stats.ap[0:16, 64:65], None, ALU.is_gt),
                  deps=[ti, tc_] + C.sut16.wdeps())
    C.sut16.wrote(ts16)
    C.iotat.read(ts16)
    t_o2 = p.step("vector", lambda e: e.memset(C.ones_row_f.ap, 1.0))
    C.ones_row_f.wrote(t_o2)
    C.iotat.read(tm1)

    xin_sem = p.new_sem("xin")
    xout_sem = p.new_sem("xout")
    xsv_sem = p.new_sem("xsv")
    C.epoch = []
    C.l2_contrib_toks = []

    split = 2 in layers
    pre_layers = [L for L in layers if L < 2]
    post_layers = [L for L in layers if L > 2]

    def refresh():
        C.epoch = p.barrier_all()

    def load_x(b, src_t, extra=()):
        src = src_t[b].rearrange("(tc p) d -> p tc d", p=128)
        tok = p.step("sync", lambda e, src=src: e.dma_start(
            out=C.xt.ap.rearrange("p (tc d) -> p tc d", tc=NTC), in_=src),
            deps=C.xt.wdeps() + list(extra), dma=True, sem=xin_sem)
        C.xt.wrote(tok)

    def ln_tr():
        refresh()
        C.xnt = Buf(C.xnt.ap, "XNT", C.epoch)
        emit_ln(C)
        emit_transpose_xn(C)
        refresh()

    def emit_layer(L, b):
        ln_tr()
        if L == 0:
            emit_l0_routing(C, b)
            refresh()
            emit_l0_attn(C, b)
            refresh()
            emit_l0_ffn(C, b, h_dram)
        elif L == 1:
            emit_l1(C, b)
        elif L == 3:
            if "ffn" in L3_PARTS:
                emit_l3_gates(C, b)
                refresh()
            if "attn" in L3_PARTS:
                emit_l3_attn(C, b, sc_row)
            refresh()
            if "ffn" in L3_PARTS:
                emit_l3_experts(C, b, h_dram)

    def store_y(b):
        dst = x_out[b].rearrange("(tc p) d -> p tc d", p=128)
        tok = p.step("sync", lambda e, dst=dst: e.dma_start(
            out=dst, in_=C.xt.ap.rearrange("p (tc d) -> p tc d", tc=NTC)),
            deps=C.xt.rdeps() + C.xt.wdeps(), dma=True, sem=xout_sem)
        C.xt.read(tok)

    save_toks = {}
    for b in range(B_PER):
        refresh()
        load_x(b, x_in)
        for L in pre_layers:
            emit_layer(L, b)
        if split:
            ln_tr()
            if "attn" in L2_PARTS:
                emit_l2_attn(C, b, contrib)
            dst = xsave[b].rearrange("(tc p) d -> p tc d", p=128)
            tok = p.step("sync", lambda e, dst=dst: e.dma_start(
                out=dst, in_=C.xt.ap.rearrange("p (tc d) -> p tc d", tc=NTC)),
                deps=C.xt.rdeps() + C.xt.wdeps(), dma=True, sem=xsv_sem)
            C.xt.read(tok)
            save_toks[b] = tok
        else:
            for L in post_layers:
                emit_layer(L, b)
            store_y(b)

    if split and "attn" not in L2_PARTS:
        class _T:
            pass
        C.l2_scr_ready = None
    if split and "attn" in L2_PARTS:
        csem = p.new_sem("l2coll")
        if DEV_NCORES == 1:
            ctok = p.step("gpsimd", lambda e: e.dma_start(out=gathered, in_=contrib),
                          deps=C.l2_contrib_toks, dma=True, sem=csem)
        else:
            ctok = p.step("gpsimd", lambda e: e.collective_compute(
                "AllToAll", ALU.bypass, replica_groups=[list(range(DEV_NCORES))],
                ins=[contrib.opt()], outs=[gathered.opt()]),
                deps=C.l2_contrib_toks, sem=csem)
        ssem = p.new_sem("l2scr")
        BT = DEV_NCORES * B_PER
        a_r = 128 // BT
        lpb = T // BT
        stok = ctok
        for il in range(B_PER):
            gsl = gathered[:, :, il * lpb:(il + 1) * lpb, :].rearrange(
                "c w (tc a) d -> tc a c w d", tc=NTC)
            dst = scr2[il].rearrange("(tc a c w) d -> tc a c w d", tc=NTC, a=a_r, c=DEV_NCORES)
            stok = p.step("sync", lambda e, gsl=gsl, dst=dst: e.dma_start(out=dst, in_=gsl),
                          deps=[stok], dma=True, sem=ssem)
        C.l2_scr_ready = stok
        for b in range(B_PER):
            refresh()
            load_x(b, xsave, extra=[save_toks[b]])
            ln_tr()
            if "attn" in L2_PARTS:
                emit_l2_scradd(C, b, scr2)
                refresh()
            if "ffn" in L2_PARTS:
                emit_l2_ffn(C, b, h_dram)
            for L in post_layers:
                emit_layer(L, b)
            store_y(b)

    p.step("sync", lambda e: e.nop(), deps=[Tok(xout_sem, p.cnt[xout_sem])])

    with allow_ici_all_to_all():
        p.build(nc)
    stack.close()
    return nc


def _prep_weights(inputs, layers):
    w = {}
    if 0 in layers:
        w["L0_Wqkv_t"] = np.ascontiguousarray(inputs["L0_attn_W_qkv"].T)
        w["L0_Wout_t"] = np.ascontiguousarray(inputs["L0_attn_W_out"].T)
        w["L0_cent_t"] = np.ascontiguousarray(inputs["L0_ffn_centroid"].T)
        w["L0_Wg"] = np.ascontiguousarray(inputs["L0_ffn_Wg"])
        bgp = np.zeros((2816, 8), np.float32); bgp[:2736] = inputs["L0_ffn_bg"].T
        w["L0_bg_t"] = bgp
        w["L0_W1"] = np.ascontiguousarray(inputs["L0_ffn_W1"])
        b1p = np.zeros((2816, 8), np.float32); b1p[:2736] = inputs["L0_ffn_b1"].T
        w["L0_b1_t"] = b1p
        w["L0_W2"] = np.ascontiguousarray(inputs["L0_ffn_W2"])
        w["L0_b2"] = np.ascontiguousarray(inputs["L0_ffn_b2"])
    if 1 in layers:
        w["L1_wcq"] = np.ascontiguousarray(inputs["L1_attn_w_cq"])
        w["L1_gq"] = np.ascontiguousarray(inputs["L1_attn_g_qnorm"].reshape(1, 96))
        w["L1_gkv"] = np.ascontiguousarray(inputs["L1_attn_g_kvnorm"].reshape(1, 32))
        w["L1_wdqn"] = np.ascontiguousarray(inputs["L1_attn_w_dq_nope"])
        w["L1_wdqr"] = _rope_perm_q(inputs["L1_attn_w_dq_rope"])
        w["L1_wckv"] = np.ascontiguousarray(inputs["L1_attn_w_ckv"])
        w["L1_wdkn"] = np.ascontiguousarray(inputs["L1_attn_w_dk_nope"])
        w["L1_wdv"] = np.ascontiguousarray(inputs["L1_attn_w_dv"])
        w["L1_wkr"] = np.ascontiguousarray(inputs["L1_attn_w_krope"])
        w["L1_wimp"] = np.ascontiguousarray(inputs["L1_attn_w_imp"].reshape(8, 128).T)
        w["L1_wselk_n"], w["L1_wselk_r"] = _rope_split_perm(inputs["L1_attn_w_selk"])
        w["L1_wselv"] = np.ascontiguousarray(inputs["L1_attn_w_selv"])
        w["L1_wink_n"], w["L1_wink_r"] = _rope_split_perm(inputs["L1_attn_w_wink"])
        w["L1_winv"] = np.ascontiguousarray(inputs["L1_attn_w_winv"])
        w["L1_wgate"] = np.ascontiguousarray(inputs["L1_attn_w_gate"])
        w["L1_wproj"] = np.ascontiguousarray(inputs["L1_attn_w_proj"])
        w["L1_rmT"] = np.ascontiguousarray(inputs["L1_ffn_rail_maker"].transpose(0, 1, 3, 2))
        w["L1_rupT"] = np.ascontiguousarray(inputs["L1_ffn_rail_up_proj"].transpose(0, 1, 3, 2))
        w["L1_wcproj_t"] = np.ascontiguousarray(inputs["L1_ffn_w_cproj"].T)
    if 2 in layers:
        for nm in ("q", "k", "v", "o"):
            w[f"L2_W{nm}"] = np.ascontiguousarray(inputs[f"L2_attn_W{nm}"])
            w[f"L2_b{nm}"] = np.ascontiguousarray(inputs[f"L2_attn_b{nm}"].reshape(1, D))
        w["L2_wg_t"] = np.ascontiguousarray(inputs["L2_ffn_w_gate"].T)
        w["L2_wu_t"] = np.ascontiguousarray(inputs["L2_ffn_w_up"].T)
        w["L2_wd_t"] = np.ascontiguousarray(inputs["L2_ffn_w_down"].T)
    if 3 in layers:
        w["L3_gs_t"] = np.ascontiguousarray(inputs["L3_attn_proj_matrix"].T)
        w["L3_iw_t"] = np.ascontiguousarray(inputs["L3_attn_input_weights"].T)
        w["L3_xis_c"] = np.ascontiguousarray(inputs["L3_attn_xis"].reshape(8, 128).T)
        w["L3_rt_t"] = np.ascontiguousarray(inputs["L3_ffn_router_w"].T)
        w["L3_w1"] = np.ascontiguousarray(inputs["L3_ffn_w1"])
        w["L3_b1_t"] = np.ascontiguousarray(inputs["L3_ffn_b1"].T)
        w["L3_w2"] = np.ascontiguousarray(inputs["L3_ffn_w2"])
        w["L3_b2"] = np.ascontiguousarray(inputs["L3_ffn_b2"])
    return w


def _rope_split_perm(w, nh=16):
    inn = w.shape[0]
    wn = np.empty((inn, nh * 32), np.float32)
    wr = np.empty((inn, nh * 64), np.float32)
    for h in range(nh):
        g, hl = h // 4, h % 4
        wn[:, 32 * h:32 * (h + 1)] = w[:, 96 * h:96 * h + 32]
        wr[:, 256 * g + 32 * hl:256 * g + 32 * hl + 32] = w[:, 96 * h + 32:96 * h + 64]
        wr[:, 256 * g + 128 + 32 * hl:256 * g + 128 + 32 * hl + 32] = w[:, 96 * h + 64:96 * h + 96]
    return np.ascontiguousarray(wn), np.ascontiguousarray(wr)


def _rope_perm_q(w, nh=16):
    out = np.empty_like(w)
    for h in range(nh):
        g, hl = h // 4, h % 4
        out[:, 256 * g + 32 * hl:256 * g + 32 * hl + 32] = w[:, 64 * h:64 * h + 32]
        out[:, 256 * g + 128 + 32 * hl:256 * g + 128 + 32 * hl + 32] = w[:, 64 * h + 32:64 * h + 64]
    return np.ascontiguousarray(out)


def kernel(**inputs):
    layers = list(DEV_LAYERS)
    w = _prep_weights(inputs, layers)
    nc = build_program(layers)
    x = np.asarray(inputs["L0_attn_x"])
    in_maps = []
    for c in range(DEV_NCORES):
        m = {"x": np.ascontiguousarray(x[B_PER * c:B_PER * (c + 1)])}
        m.update(w)
        in_maps.append(m)
    res = run_bass_kernel_spmd(nc, in_maps, core_ids=list(range(DEV_NCORES)))
    return np.concatenate([np.asarray(r["y"]) for r in res.results], axis=0)
